# version 02
# baseline (speedup 1.0000x reference)
import contextlib
import numpy as np
import ml_dtypes
import concourse.bass as bass
import concourse.mybir as mybir
from concourse.bass_utils import run_bass_kernel_spmd

F32 = mybir.dt.float32
BF16 = mybir.dt.bfloat16
AF = mybir.ActivationFunctionType
ALU = mybir.AluOpType
AX = mybir.AxisListType

D = 1024
S = 4096
QT = 1024
NQ = S // QT
KC = D // 128
XNW = QT + 256
NE0 = 12
FE = 24
EPS = 1e-5

FUSED = True


class Em:
    ENGS = ("sp", "gp", "pe", "act", "dve")
    DMA_SEMS = ("dio", "dwm", "sx", "st", "wW", "s1a", "s1b", "w2s",
                "e1a", "e1b", "e2a", "e2b", "be2", "cb", "cc",
                "ua", "ub", "da", "db", "v2")
    SEMS = ("pe", "act", "dve") + DMA_SEMS

    def __init__(self, limit=None):
        self.ops = {e: [] for e in self.ENGS}
        self.cnt = dict.fromkeys(self.SEMS, 0)
        self.wm = {}
        self.seq = 0
        self.limit = limit

    def _filter(self, eng, waits):
        out = []
        stack = list(waits)
        while stack:
            t = stack.pop()
            if t is None:
                continue
            if isinstance(t, list):
                stack.extend(t)
                continue
            s, v = t
            if v <= self.wm.get((eng, s), 0):
                continue
            self.wm[(eng, s)] = v
            out.append((s, v))
        return out

    def op(self, eng, fn, waits=(), inc=None, n=1):
        self.seq += 1
        if self.limit is not None and self.seq > self.limit:
            return (inc, self.cnt[inc]) if inc else None
        ws = self._filter(eng, waits)
        tok = None
        if inc:
            self.cnt[inc] += n
            tok = (inc, self.cnt[inc])

        def replay(eo, sems, _ws=ws, _fn=fn, _inc=inc, _n=n):
            for s, v in _ws:
                eo.wait_ge(sems[s], v)
            ins = _fn(eo)
            if _inc:
                ins.then_inc(sems[_inc], _n)

        self.ops[eng].append(replay)
        return tok

    def dma(self, eng, out_ap, in_ap, waits=(), sem=None):
        if sem is None:
            sem = "dwm" if eng == "gp" else "dio"
        return self.op(eng, lambda e, o=out_ap, i=in_ap: e.dma_start(out=o, in_=i),
                       waits, inc=sem, n=16)

    def snap(self):
        return [(s, v) for s, v in self.cnt.items() if v > 0]

    def barrier_all(self, engs=None, always=False):
        snap = self.snap()
        for eng in (engs or self.ENGS):
            ws = self._filter(eng, snap)
            if ws:
                def rp(eo, sems, _ws=ws):
                    for s, v in _ws:
                        eo.wait_ge(sems[s], v)
                self.ops[eng].append(rp)

    def final_wait(self, eng, sem):
        v = self.cnt[sem]
        fw = lambda eo, sems, s=sem, vv=v: eo.wait_ge(sems[s], vv)
        self.ops[eng].append(fw)


class Bank:
    def __init__(self, ap):
        self.ap = ap
        self.free = None
        self.ready = None


def pe_group(em, bank, items, waits=(), ready=True):
    n = len(items)
    tok = None
    for i, (tgt, l, r) in enumerate(items):
        w = list(waits) + [bank.free] if i == 0 else ()
        tok = em.op(
            "pe",
            (lambda e, t=tgt, l=l, r=r, st=(i == 0), sp=(i == n - 1):
             e.matmul(t, l, r, start=st, stop=sp)),
            w, inc="pe" if i == n - 1 else None)
    if ready:
        bank.ready = tok
    return tok


def pe_inter(em, banks, lhs_fn, rhs_fn, waits=(), nk=8):
    for k in range(nk):
        for i, bk in enumerate(banks):
            w = (list(waits) + [bk.free]) if k == 0 else ()
            tok = em.op(
                "pe",
                (lambda e, t=bk.ap[:], l=lhs_fn(k), r=rhs_fn(k, i),
                 st=(k == 0), sp=(k == nk - 1):
                 e.matmul(t, l, r, start=st, stop=sp)),
                w, inc="pe" if k == nk - 1 else None)
            if k == nk - 1:
                bk.ready = tok


def ttiles(t0, t1, w=512):
    out = []
    t = t0
    while t < t1:
        out.append((t, min(w, t1 - t)))
        t += w
    return out


def build_nc(layers=("L0", "L1"), nquarters=NQ, detect_races=True, ph=99, trunc=None,
             dump=False):
    nc = bass.Bass("TRN2", target_bir_lowering=False,
                   detect_race_conditions=detect_races)
    dt_in = {}

    def din(name, shape, dt=F32):
        dt_in[name] = nc.dram_tensor(name, list(shape), dt, kind="ExternalInput").ap()
        return dt_in[name]

    src = din("src", (S, D))
    dst = nc.dram_tensor("dst", [S, D], F32, kind="ExternalOutput").ap()

    wq = din("wq", (D, D)); wk = din("wk", (D, D)); wv = din("wv", (D, D)); wo = din("wo", (D, D))
    ws1 = din("ws1", (D, 4096)); ws2 = din("ws2", (2048, D))
    we1 = din("we1", (NE0, D, 6144)); we2 = din("we2", (NE0, 3072, D))
    wr = din("wr", (D, NE0))
    bq_c = din("bq_c", (128, 8)); bk_c = din("bk_c", (128, 8))
    bs1_c = din("bs1_c", (128, 32)); be1_c = din("be1_c", (128, 576))
    bv_r = din("bv_r", (1, D)); bo_r = din("bo_r", (1, D)); bs2_r = din("bs2_r", (1, D))
    br_r = din("br_r", (1, NE0)); be2_d = din("be2", (NE0, D))
    cbind = din("cbind", (D, D)); ccorr = din("ccorr", (D, D))
    supt = din("supt", (2, D, 512)); sdnt = din("sdnt", (2, 512, D))
    rupt = din("rupt", (16, D, 512)); rdnt = din("rdnt", (16, 512, D))
    rwt = din("rwt", (D, 16))
    decay_rep = din("decay_rep", (128, 1)); gate_rep = din("gate_rep", (128, 1))

    ident_d = nc.inline_tensor(np.eye(128, dtype=np.float32), name="ident")
    counts = np.ones(S, np.float32) * 2.0
    counts[:64] = 1.0
    counts[-64:] = 1.0
    rcw_np = np.zeros((128, 63), np.float32)
    for w in range(63):
        rcw_np[:, w] = 1.0 / counts[64 * w:64 * w + 128]
    rcw_d = nc.inline_tensor(rcw_np, name="rcw")
    r64_np = np.roll(np.eye(128, dtype=np.float32), 64, axis=1)
    r64f_d = nc.inline_tensor(r64_np, name="r64f")

    em = Em(limit=trunc)

    with contextlib.ExitStack() as ctx:
        E = ctx.enter_context

        acc = E(nc.sbuf_tensor("acc", [128, 8 * 1024], F32))
        xn = E(nc.sbuf_tensor("xn", [128, KC * XNW], BF16))
        arena = E(nc.sbuf_tensor("arena", [128, 57344], BF16))
        f32p = E(nc.sbuf_tensor("f32p", [128, 4 * 1024], F32))
        dtile = E(nc.sbuf_tensor("dtile", [128, 1024], F32))
        stats = E(nc.sbuf_tensor("stats", [128, 64], F32))
        wstat = E(nc.sbuf_tensor("wstat", [128, 16], F32))
        wtok = E(nc.sbuf_tensor("wtok", [128, 96], F32))
        gates = E(nc.sbuf_tensor("gates", [128, 128], F32))
        r_scr = E(nc.sbuf_tensor("r_scr", [128, 16], F32))
        m8 = E(nc.sbuf_tensor("m8", [128, 8], F32))
        eqA = E(nc.sbuf_tensor("eqA", [128, 16], F32))
        eqB = E(nc.sbuf_tensor("eqB", [128, 16], F32))
        carries = E(nc.sbuf_tensor("carries", [128, 8], F32))
        dcol = E(nc.sbuf_tensor("dcol", [128, 2], F32))
        epsc = E(nc.sbuf_tensor("epsc", [128, 1], F32))
        p32 = E(nc.sbuf_tensor("p32", [128, 2 * 128], F32))
        pT_sb = E(nc.sbuf_tensor("pT_sb", [128, 2 * 128], BF16))
        gt_scr = E(nc.sbuf_tensor("gt_scr", [128, 2 * 512], BF16))
        at_scr = E(nc.sbuf_tensor("at_scr", [128, 2 * 512], BF16))
        bqk_sb = E(nc.sbuf_tensor("bqk_sb", [128, 16], F32))
        bs1_sb = E(nc.sbuf_tensor("bs1_sb", [128, 32], F32))
        be1_sb = E(nc.sbuf_tensor("be1_sb", [128, 576], F32))
        rcw_sb = E(nc.sbuf_tensor("rcw_sb", [128, 63], F32))
        ident_sb = E(nc.sbuf_tensor("ident_sb", [128, 128], F32))
        r64f_sb = E(nc.sbuf_tensor("r64f_sb", [128, 128], F32))
        cp32 = E(nc.sbuf_tensor("cp32", [128, 2 * 128], F32))
        wr_sb = E(nc.sbuf_tensor("wr_sb", [128, 96], BF16))
        rwt_sb = E(nc.sbuf_tensor("rwt_sb", [128, 128], BF16))
        drep = E(nc.sbuf_tensor("drep", [128, 1], F32))
        grep = E(nc.sbuf_tensor("grep", [128, 1], F32))
        ones_bf = E(nc.sbuf_tensor("ones_bf", [1, 128], BF16))
        bv_sb = E(nc.sbuf_tensor("bv_sb", [1, D], BF16))
        bo_sb = E(nc.sbuf_tensor("bo_sb", [1, D], BF16))
        bs2_sb = E(nc.sbuf_tensor("bs2_sb", [1, D], BF16))
        br_sb = E(nc.sbuf_tensor("br_sb", [1, NE0], BF16))
        be2_sb = E(nc.sbuf_tensor("be2_sb", [NE0, D], BF16))
        be2row = E(nc.sbuf_tensor("be2row", [1, D], BF16))

        P = [Bank(E(nc.psum_tensor(f"P{i}", [128, 512], F32))) for i in range(8)]
        spacer = E(nc.sbuf_tensor("spacer", [128, 2], F32))

        def dsp():
            em.op("dve", lambda e: e.tensor_scalar(spacer[:, 0:1], spacer[:, 0:1],
                                                   1.0, None, ALU.mult))

        def asp():
            em.op("act", lambda e: e.activation(spacer[:, 1:2], spacer[:, 1:2],
                                                AF.Identity, bias=0.0, scale=1.0))

        sems = {}
        for sname in Em.SEMS:
            sems[sname] = E(nc.semaphore(sname))
        block = E(nc.Block())

        def ar(c0, w):
            return arena[:, c0:c0 + w]
        A_QT, A_KT, A_VT, A_V2 = 0, 10240, 20480, 30720
        A_OT, A_WA = 39936, 48128
        A_ACT = 0
        A_WS2 = 16384
        A_E2 = 24576
        A_E1A, A_E1B = 49152, 53248
        A_CB, A_CC, A_HOLO = 0, 8192, 16384
        A_ACTR = 0
        A_SUPA, A_SUPB = 4096, 8192
        A_SDNA, A_SDNB = 12288, 16384 + 4096

        def xn_sl(k, t0, w):
            return xn[:, k * XNW + t0:k * XNW + t0 + w]

        def ln_chunks(q, with_halo):
            lo = 1 if (q > 0 and with_halo) else 0
            hi = 1 if (q < 3 and with_halo) else 0
            ch = []
            if lo:
                ch.append((f32p[:, 0:1024], 0))
            for c in range(8):
                ch.append((acc[:, c * 1024:(c + 1) * 1024], 128 + c * 128))
            if hi:
                ch.append((f32p[:, 1024:2048], 1152))
            return ch

        def emit_ln(q, chunks, tok_srcs, update_acc):
            last = None
            tp_done = {}
            for idx, (src, t_loc) in enumerate(chunks):
                sc = stats[:, idx * 6:idx * 6 + 6]
                par = idx % 2
                tsrc = tok_srcs[idx] if idx < len(tok_srcs) else None
                xtsq = f32p[:, 2 * 1024:3 * 1024]
                em.op("dve", lambda e, o=xtsq, i=src:
                      e.tensor_tensor(o, i, i, ALU.mult),
                      [tsrc, tp_done.get(0)])
                em.op("dve", lambda e, o=sc[:, 0:1], i=src: e.reduce_sum(o, i, AX.X))
                em.op("dve", lambda e, o=sc[:, 2:3], i=xtsq:
                      e.reduce_sum(o, i, AX.X))
                em.op("dve", lambda e, o=sc[:, 1:2], i=sc[:, 0:1]:
                      e.tensor_scalar(o, i, -1.0 / D, None, ALU.mult))
                em.op("dve", lambda e, o=sc[:, 4:5], i=sc[:, 2:3]:
                      e.tensor_scalar(o, i, 1.0 / D, None, ALU.mult))
                em.op("dve", lambda e, o=sc[:, 3:4], i=sc[:, 1:2]:
                      e.tensor_scalar(o, i, i, None, ALU.mult))
                dsp()
                tvar = em.op("dve", lambda e, o=sc[:, 3:4], i=sc[:, 4:5], m=sc[:, 3:4]:
                             e.tensor_scalar(o, i, m, None, ALU.subtract),
                             (), inc="dve")
                em.op("act", lambda e, o=sc[:, 3:4], i=sc[:, 3:4]:
                      e.activation(o, i, AF.Ln, bias=epsc[:], scale=1.0), [tvar])
                asp()
                trstd = em.op("act", lambda e, o=sc[:, 3:4], i=sc[:, 3:4]:
                              e.activation(o, i, AF.Exp, bias=0.0, scale=-0.5),
                              inc="act")
                xt = f32p[:, 2 * 1024:3 * 1024]
                txt = em.op("dve", lambda e, o=xt, i=src, m=sc[:, 1:2], r=sc[:, 3:4]:
                            e.tensor_scalar(o, i, m, r, ALU.add, ALU.mult),
                            [trstd, tp_done.get(0)], inc="dve")
                ttp = None
                for j in range(KC):
                    bk = P[4 + (j % 4)]
                    sub = bk.ap[:, 0:128]
                    ttp = em.op("pe", lambda e, o=sub, i=xt[:, j * 128:(j + 1) * 128]:
                                e.transpose(o, i, ident_sb[:]), [txt, bk.free],
                                inc="pe")
                    bk.ready = ttp
                    tev = em.op("dve", lambda e, o=xn_sl(j, t_loc, 128), i=sub:
                                e.tensor_copy(o, i), [ttp], inc="dve")
                    bk.free = tev
                    last = tev
                tp_done[0] = ttp
                if update_acc:
                    em.op("dve", lambda e, o=sc[:, 0:1], i=sc[:, 3:4]:
                          e.tensor_scalar(o, i, 1.0, None, ALU.add))
                    em.op("dve", lambda e, o=sc[:, 1:2], i=sc[:, 1:2], r=sc[:, 3:4]:
                          e.tensor_scalar(o, i, r, None, ALU.mult))
                    dsp()
                    last = em.op("dve", lambda e, o=src, i=src, a=sc[:, 0:1],
                                 b=sc[:, 1:2]:
                                 e.tensor_scalar(o, i, a, b, ALU.mult, ALU.add),
                                 inc="dve")
            return last

        def down_proj(name, nf, act_off, wb_ap, tok_w, evac, extra_waits=(),
                      group_starts=None):
            for p2 in range(2):
                for f in range(nf):
                    for sub in range(4):
                        for n in range(2):
                            bk = P[sub * 2 + n]
                            tgt = bk.ap[:]
                            l = ar(act_off + f * 1024 + p2 * 512 + sub * 128, 128)
                            r = wb_ap(f, n)
                            first = (f == 0)
                            w = (list(extra_waits) + [tok_w(f), bk.free]) if first else \
                                ([tok_w(f)] if (group_starts and f in group_starts) else ())
                            em.op("pe", lambda e, t=tgt, l=l, r=r, st=first:
                                  e.matmul(t, l, r, start=st, stop=False), w)
                for sub in range(4):
                    for n in range(2):
                        bk = P[sub * 2 + n]
                        yield ("close", p2, sub, n, bk)

        em.op("dve", lambda e: e.memset(spacer[:], 0.0))
        em.op("dve", lambda e: e.memset(ones_bf[:], 1.0))
        em.op("dve", lambda e: e.memset(epsc[:], EPS))
        em.op("dve", lambda e: e.memset(xn[:], 0.0))
        em.op("dve", lambda e: e.memset(arena[:], 0.0))
        em.op("dve", lambda e: e.memset(carries[:], 0.0), inc="dve")
        t_ident = em.dma("sp", ident_sb[:], ident_d.ap())
        em.dma("sp", r64f_sb[:], r64f_d.ap())
        t_rcw = em.dma("sp", rcw_sb[:], rcw_d.ap())
        em.dma("sp", bqk_sb[:, 0:8], bq_c)
        em.dma("sp", bqk_sb[:, 8:16], bk_c)
        em.dma("sp", bs1_sb[:], bs1_c)
        em.dma("sp", be1_sb[:], be1_c)
        t_drep = em.dma("sp", drep[:], decay_rep)
        t_grep = em.dma("sp", grep[:], gate_rep)
        em.dma("gp", bv_sb[:], bv_r)
        em.dma("gp", bo_sb[:], bo_r)
        em.dma("gp", bs2_sb[:], bs2_r)
        em.dma("gp", br_sb[:], br_r)
        em.dma("gp", be2_sb[:], be2_d)
        em.dma("gp", wr_sb[:].rearrange("p (k e) -> p k e", e=NE0),
               wr.rearrange("(k p) e -> p k e", p=128))
        t_rwt = em.dma("gp", rwt_sb[:].rearrange("p (k e) -> p k e", e=16),
                       rwt.rearrange("(k p) e -> p k e", p=128))
        ta = em.op("act", lambda e: e.activation(dcol[:, 0:1], drep[:], AF.Exp,
                                                 bias=0.0, scale=-1.0),
                   [("dio", em.cnt["dio"])], inc="act")
        tb = em.op("dve", lambda e: e.tensor_scalar(dcol[:, 0:1], dcol[:, 0:1], 1.0,
                                                    None, ALU.add), [ta])
        dsp()
        tc_ = em.op("dve", lambda e: e.reciprocal(dcol[:, 1:2], dcol[:, 0:1]))
        em.op("dve", lambda e: e.memset(dtile[:], 1.0))
        dsp()
        t_dtile = em.op("dve", lambda e: e.tensor_scalar(dtile[:], dtile[:],
                                                         dcol[:, 1:2], None, ALU.mult),
                        inc="dve")

        for q in range(nquarters):
            qs = q * QT
            lo = 1 if q > 0 else 0
            hi = 1 if q < 3 else 0
            frame0 = qs - 128
            em.barrier_all()
            barrier = em.snap()

            em.dma("sp",
                   acc[:].rearrange("p (c n) -> p c n", n=1024),
                   src[qs:qs + QT, :].rearrange("(c p) n -> p c n", p=128),
                   barrier, sem="sx")
            if "L0" in layers:
                if lo:
                    em.dma("sp", f32p[:, 0:1024],
                           src[qs - 128:qs, :], barrier, sem="sx")
                if hi:
                    em.dma("sp", f32p[:, 1024:2048],
                           src[qs + QT:qs + QT + 128, :], barrier, sem="sx")
            t_main = t_hx0 = t_hx1 = ("sx", em.cnt["sx"])

            if "L0" in layers:
                chunks = ln_chunks(q, True)
                toks = []
                for (srcap, t_loc) in chunks:
                    if t_loc == 0:
                        toks.append(t_hx0)
                    elif t_loc == 1152:
                        toks.append(t_hx1)
                    else:
                        toks.append(t_main)
                t_xn = emit_ln(q, chunks, toks, update_acc=False)

                tmin = 0 if lo else 128
                tmax = 1280 if hi else 1152
                t_wtok = ("dve", em.cnt["dve"])

            if "L0" in layers and ph >= 1:
                t_wq = em.dma("gp", ar(A_WA, 8192).rearrange("p (k m) -> p k m", m=1024),
                              wq.rearrange("(k p) m -> p k m", p=128), [t_ident], sem="wW")
                tiles = ttiles(tmin, tmax)

                def proj_fm(dst_off, t_w, bias_base, waits_extra):
                    tok_mm_last = None
                    for m in range(KC):
                        bks = [P[(m % 2) * 3 + i] for i in range(len(tiles))]
                        for ti, (t0, wdt) in enumerate(tiles):
                            bk = bks[ti]
                            items = []
                            for k in range(KC):
                                items.append((bk.ap[:, 0:wdt],
                                              ar(A_WA + k * 1024 + m * 128, 128),
                                              xn_sl(k, t0, wdt)))
                            tok = pe_group(em, bk, items,
                                           [t_w, t_xn] + list(waits_extra))
                            tok_mm_last = tok
                            tev = em.op("act",
                                        lambda e, o=ar(dst_off + m * XNW + t0, wdt),
                                        i=bk.ap[:, 0:wdt],
                                        b=bqk_sb[:, bias_base + m:bias_base + m + 1]:
                                        e.activation(o, i, AF.Identity, bias=b, scale=1.0),
                                        [tok], inc="act")
                            bk.free = tev
                    return tok_mm_last

                tql = proj_fm(A_QT, t_wq, 0, [])
                t_wk = em.dma("gp", ar(A_WA, 8192).rearrange("p (k m) -> p k m", m=1024),
                              wk.rearrange("(k p) m -> p k m", p=128), [("pe", tql[1])], sem="wW")
                tkl = proj_fm(A_KT, t_wk, 8, [])
                tqk = (("act", em.cnt["act"]))

                t_wv = em.dma("gp", ar(A_WA, 8192).rearrange("p (k m) -> p k m", m=1024),
                              wv.rearrange("(k p) m -> p k m", p=128), [("pe", tkl[1])], sem="wW")
                vc0, vc1 = tmin // 128, tmax // 128
                for vc in range(vc0, vc1):
                    par = vc % 2
                    bks = (P[par * 2], P[par * 2 + 1])
                    seq = {0: [], 1: []}
                    for k in range(KC):
                        for n in range(2):
                            seq[n].append((bks[n].ap[:],
                                           xn_sl(k, vc * 128, 128),
                                           ar(A_WA + k * 1024 + n * 512, 512)))
                    for n in range(2):
                        bk = bks[n]
                        items = seq[n]
                        ntot = len(items) + 1
                        for i, (tgt, l, r) in enumerate(items):
                            w = [t_wv, t_xn, bk.free] if i == 0 else ()
                            em.op("pe", lambda e, t=tgt, l=l, r=r, st=(i == 0):
                                  e.matmul(t, l, r, start=st, stop=False), w)
                        tok = em.op("pe", lambda e, t=bk.ap[:],
                                    r=bv_sb[0:1, n * 512:(n + 1) * 512]:
                                    e.matmul(t, ones_bf[0:1, :], r, start=False,
                                             stop=True), (), inc="pe")
                        bk.ready = tok
                        tev = em.op("dve", lambda e,
                                    o=ar(A_VT + vc * 1024 + n * 512, 512),
                                    i=bk.ap[:]: e.tensor_copy(o, i), [tok], inc="dve")
                        bk.free = tev
                t_v_all = ("dve", em.cnt["dve"])
                em.dma("gp", ar(A_V2, 9216)[0:64, :], ar(A_VT, 9216)[64:128, :],
                       [t_v_all], sem="v2")
                em.dma("gp", ar(A_V2, 9216)[64:128, :],
                       ar(A_VT + 1024, 9216)[0:64, :], [t_v_all], sem="v2")
                t_v2 = ("v2", em.cnt["v2"])

                t_ot0 = em.op("dve", lambda e: e.memset(ar(A_OT, 8192), 0.0),
                              [tqk], inc="dve")
                s_first = max(0, qs - 64)
                s_last = min(S - 128, qs + QT - 64)
                t_scores_last = None
                pf_war = [None, None]
                ws_war = [None, None]
                for wi, s in enumerate(range(s_first, s_last + 1, 64)):
                    par = wi % 2
                    t_l = s - frame0
                    w_glob = s // 64
                    off = t_l % 128
                    vc = t_l // 128
                    bkS, bkT = P[4 + par], P[6 + par]
                    items = []
                    for k in range(KC):
                        items.append((bkS.ap[:, 0:128],
                                      ar(A_QT + k * XNW + t_l, 128),
                                      ar(A_KT + k * XNW + t_l, 128)))
                    tok_s = pe_group(em, bkS, items, [tqk])
                    t_scores_last = tok_s
                    ws = wstat[:, par * 8:par * 8 + 8]
                    tmx = em.op("dve", lambda e, o=ws[:, 0:1], i=bkS.ap[:, 0:128]:
                                e.reduce_max(o, i, AX.X, negate=True),
                                [tok_s, ws_war[par]], inc="dve")
                    dsp()
                    em.op("dve", lambda e, o=ws[:, 1:2], i=ws[:, 0:1]:
                          e.tensor_scalar(o, i, 0.125, None, ALU.mult), inc="dve")
                    tdv = ("dve", em.cnt["dve"])
                    pf = p32[:, par * 128:(par + 1) * 128]
                    texp = em.op("act", lambda e, o=pf, i=bkS.ap[:, 0:128],
                                 b=ws[:, 1:2]:
                                 e.activation(o, i, AF.Exp, bias=b, scale=0.125),
                                 [tdv, t_rcw, pf_war[par]], inc="act")
                    bkS.free = texp
                    ws_war[par] = texp
                    em.op("dve", lambda e, o=ws[:, 2:3], i=pf:
                          e.reduce_sum(o, i, AX.X), [texp])
                    dsp()
                    em.op("dve", lambda e, o=ws[:, 3:4], i=ws[:, 2:3]:
                          e.reciprocal(o, i))
                    dsp()
                    em.op("dve", lambda e, o=ws[:, 3:4], i=ws[:, 3:4],
                          r=rcw_sb[:, w_glob:w_glob + 1]:
                          e.tensor_scalar(o, i, r, None, ALU.mult))
                    dsp()
                    tpn = em.op("dve", lambda e, o=pf, i=pf, r=ws[:, 3:4]:
                                e.tensor_scalar(o, i, r, None, ALU.mult), inc="dve")
                    idmat = ident_sb if off == 0 else r64f_sb
                    ttp = em.op("pe", lambda e, o=bkT.ap[:, 0:128], i=pf, m=idmat:
                                e.transpose(o, i, m[:]), [tpn, bkT.free],
                                inc="pe")
                    bkT.ready = ttp
                    pf_war[par] = ttp
                    pts = pT_sb[:, par * 128:(par + 1) * 128]
                    tpt = em.op("dve", lambda e, o=pts, i=bkT.ap[:, 0:128]:
                                e.tensor_copy(o, i), [ttp], inc="dve")
                    bkT.free = tpt
                    psO = (P[0 + 2 * par], P[1 + 2 * par])
                    for n in range(2):
                        bk = psO[n]
                        if off == 0:
                            vsl = ar(A_VT + vc * 1024 + n * 512, 512)
                        else:
                            vsl = ar(A_V2 + vc * 1024 + n * 512, 512)
                        items = [(bk.ap[:], pts, vsl)]
                        tok_o = pe_group(em, bk, items, [tpt, t_v_all, t_v2])
                        if off == 0:
                            c = (s - qs) // 128
                            ov = ar(A_OT + c * 1024 + n * 512, 512)
                            pv = bk.ap[:]
                            tsc = em.op("dve", lambda e, o=ov, i=pv:
                                        e.tensor_tensor(o, o, i, ALU.add),
                                        [tok_o, t_ot0], inc="dve")
                        else:
                            cA = (s - qs - 64) // 128
                            tsc = None
                            if cA >= 0:
                                ov = ar(A_OT + cA * 1024 + n * 512, 512)[64:128, :]
                                pv = bk.ap[:][64:128, :]
                                tsc = em.op("dve", lambda e, o=ov, i=pv:
                                            e.tensor_tensor(o, o, i, ALU.add),
                                            [tok_o, t_ot0],
                                            inc="dve")
                            if cA + 1 <= 7:
                                ov = ar(A_OT + (cA + 1) * 1024 + n * 512, 512)[0:64, :]
                                pv = bk.ap[:][0:64, :]
                                tsc = em.op("dve", lambda e, o=ov, i=pv:
                                            e.tensor_tensor(o, o, i, ALU.add),
                                            [tok_o, t_ot0],
                                            inc="dve")
                        bk.free = tsc
                t_scatter_all = ("dve", em.cnt["dve"])

                t_wo = em.dma("gp", ar(A_WA, 8192).rearrange("p (k m) -> p k m", m=1024),
                              wo.rearrange("(k p) m -> p k m", p=128),
                              [t_scores_last], sem="wW")
                cp_war = [None, None]
                for tc in range(8):
                    for k in range(KC):
                        bkt = P[4 + (k % 2)]
                        cps = cp32[:, (k % 2) * 128:(k % 2 + 1) * 128]
                        tcp = em.op("dve", lambda e, o=cps,
                                    i=ar(A_OT + tc * 1024 + k * 128, 128):
                                    e.tensor_copy(o, i),
                                    [t_scatter_all, cp_war[k % 2]], inc="dve")
                        ttp = em.op("pe", lambda e, o=bkt.ap[:, 0:128], i=cps:
                                    e.transpose(o, i, ident_sb[:]),
                                    [tcp, bkt.free], inc="pe")
                        cp_war[k % 2] = ttp
                        tev = em.op("dve", lambda e,
                                    o=ar(A_QT + k * 1024 + tc * 128, 128),
                                    i=bkt.ap[:, 0:128]:
                                    e.tensor_copy(o, i), [ttp], inc="dve")
                        bkt.free = tev
                    t_ofm = ("dve", em.cnt["dve"])
                    for n in range(2):
                        bk = P[(tc % 2) * 2 + n]
                        items = []
                        for k in range(KC):
                            items.append((bk.ap[:],
                                          ar(A_QT + k * 1024 + tc * 128, 128),
                                          ar(A_WA + k * 1024 + n * 512, 512)))
                        for i, (tgt, l, r) in enumerate(items):
                            w = [t_wo, t_ofm, bk.free] if i == 0 else ()
                            em.op("pe", lambda e, t=tgt, l=l, r=r, st=(i == 0):
                                  e.matmul(t, l, r, start=st, stop=False), w)
                        tok = em.op("pe", lambda e, t=bk.ap[:],
                                    r=bo_sb[0:1, n * 512:(n + 1) * 512]:
                                    e.matmul(t, ones_bf[0:1, :], r, start=False,
                                             stop=True), (), inc="pe")
                        bk.ready = tok
                        av = acc[:, tc * 1024 + n * 512:tc * 1024 + (n + 1) * 512]
                        tev = em.op("dve", lambda e, o=av, i=bk.ap[:]:
                                    e.tensor_tensor(o, o, i, ALU.add),
                                    [tok, t_main], inc="dve")
                        bk.free = tev

            if "L0" in layers and ph >= 2:
                prev_tex = None
                for tc in range(8):
                    bk = P[4 + tc % 2]
                    items = []
                    for k in range(KC):
                        items.append((bk.ap[:, 0:NE0],
                                      xn_sl(k, 128 + tc * 128, 128),
                                      wr_sb[:, k * NE0:(k + 1) * NE0]))
                    for i, (tgt, l, r) in enumerate(items):
                        w = [bk.free] if i == 0 else ()
                        em.op("pe", lambda e, t=tgt, l=l, r=r, st=(i == 0):
                              e.matmul(t, l, r, start=st, stop=False), w)
                    tok = em.op("pe", lambda e, t=bk.ap[:, 0:NE0],
                                r=br_sb[0:1, :]:
                                e.matmul(t, ones_bf[0:1, :], r, start=False,
                                         stop=True), (), inc="pe")
                    bk.ready = tok
                    trr = em.op("dve", lambda e, o=r_scr[:, 0:NE0], i=bk.ap[:, 0:NE0]:
                                e.tensor_copy(o, i), [tok, prev_tex], inc="dve")
                    bk.free = trr
                    dsp()
                    em.op("dve", lambda e, o=m8[:], i=r_scr[:, 0:NE0]:
                          e.max(o, i))
                    dsp()
                    tsub = em.op("dve", lambda e, o=wstat[:, 0:1], a=m8[:, 0:1],
                                 b=m8[:, 1:2]:
                                 e.tensor_scalar(o, a, b, None, ALU.subtract),
                                 inc="dve")
                    tex = em.op("act", lambda e, o=wstat[:, 1:2], i=wstat[:, 0:1]:
                                e.activation(o, i, AF.Exp, bias=0.0,
                                             scale=-1.0 / 1.5), [tsub], inc="act")
                    prev_tex = tex
                    em.op("dve", lambda e, o=wstat[:, 1:2], i=wstat[:, 1:2]:
                          e.tensor_scalar(o, i, 1.0, None, ALU.add), [tex])
                    dsp()
                    em.op("dve", lambda e, o=wstat[:, 2:3], i=wstat[:, 1:2]:
                          e.reciprocal(o, i))
                    dsp()
                    em.op("dve", lambda e, o=wstat[:, 3:4], i=wstat[:, 2:3]:
                          e.tensor_scalar(o, i, -1.0, 1.0, ALU.mult, ALU.add))
                    em.op("dve", lambda e, o=eqA[:, 0:NE0], i=r_scr[:, 0:NE0],
                          m=m8[:, 0:1]: e.tensor_scalar(o, i, m, None, ALU.is_equal))
                    dsp()
                    em.op("dve", lambda e, o=eqA[:, 0:NE0], i=eqA[:, 0:NE0],
                          s=wstat[:, 2:3]: e.tensor_scalar(o, i, s, None, ALU.mult))
                    em.op("dve", lambda e, o=eqB[:, 0:NE0], i=r_scr[:, 0:NE0],
                          m=m8[:, 1:2]: e.tensor_scalar(o, i, m, None, ALU.is_equal))
                    dsp()
                    em.op("dve", lambda e, o=wtok[:, tc * NE0:(tc + 1) * NE0],
                          i=eqB[:, 0:NE0], s=wstat[:, 3:4], a=eqA[:, 0:NE0]:
                          e.scalar_tensor_tensor(o, i, s, a, ALU.mult, ALU.add),
                          inc="dve")
                t_wtok = ("dve", em.cnt["dve"])

            if "L0" in layers and ph >= 3:
                ffn_barrier = em.snap()
                t_ws2 = em.dma("gp", ar(A_WS2, 16384).rearrange("p (f n) -> p f n", n=1024),
                               ws2.rearrange("(f p) n -> p f n", p=128), ffn_barrier, sem="w2s")
                ws1v = ws1.rearrange("(k p) (j x n) -> p j x k n", p=128, j=2, n=128)
                s1bufs = (ar(A_E1A, 2048), ar(A_E1B, 2048))
                s1free = [None, None]
                gwar = {0: None, 1: None}
                for f in range(16):
                    buf = s1bufs[f % 2]
                    t_w = em.dma("gp",
                                 buf.rearrange("p (j k n) -> p j k n", j=2, n=128),
                                 ws1v[:, :, f, :, :],
                                 [s1free[f % 2]] + (ffn_barrier if f < 2 else []),
                                 sem=("s1a", "s1b")[f % 2])
                    par = (f % 2) * 4
                    bksA = (P[par], P[par + 1])
                    bksG = (P[par + 2], P[par + 3])
                    for half, bks, jof in ((0, bksA, 0), (1, bksG, 1024)):
                        pe_inter(em, bks,
                                 (lambda k, b=buf, j=jof:
                                  b[:, j + k * 128:j + (k + 1) * 128]),
                                 (lambda k, tt: xn_sl(k, 128 + tt * 512, 512)),
                                 [t_w, t_xn])
                    for tt in range(2):
                        bkA, bkG = bksA[tt], bksG[tt]
                        gts = gt_scr[:, tt * 512:(tt + 1) * 512]
                        ats = at_scr[:, tt * 512:(tt + 1) * 512]
                        tg = em.op("act", lambda e, o=gts, i=bkG.ap[:],
                                   b=bs1_sb[:, 16 + f:17 + f]:
                                   e.activation(o, i, AF.Gelu, bias=b, scale=1.0),
                                   [bkG.ready, gwar[tt]], inc="act")
                        bkG.free = tg
                        ta2 = em.op("act", lambda e, o=ats, i=bkA.ap[:],
                                    b=bs1_sb[:, f:f + 1]:
                                    e.activation(o, i, AF.Identity, bias=b, scale=1.0),
                                    [bkA.ready, gwar[tt]], inc="act")
                        bkA.free = ta2
                        tm = em.op("dve", lambda e,
                                   o=ar(A_ACT + f * 1024 + tt * 512, 512),
                                   a=ats, g=gts:
                                   e.tensor_tensor(o, a, g, ALU.mult),
                                   [tg, ta2], inc="dve")
                        gwar[tt] = tm
                    s1free[f % 2] = gwar[1]
                t_act_all = ("dve", em.cnt["dve"])
                for p2 in range(2):
                    for f in range(16):
                        for sub in range(4):
                            for n in range(2):
                                bk = P[sub * 2 + n]
                                l = ar(A_ACT + f * 1024 + p2 * 512 + sub * 128, 128)
                                r = ar(A_WS2 + f * 1024 + n * 512, 512)
                                w = [t_ws2, t_act_all, bk.free] if f == 0 else ()
                                em.op("pe", lambda e, t=bk.ap[:], l=l, r=r,
                                      st=(f == 0):
                                      e.matmul(t, l, r, start=st, stop=False), w)
                    for sub in range(4):
                        for n in range(2):
                            bk = P[sub * 2 + n]
                            tok = em.op("pe", lambda e, t=bk.ap[:],
                                        r=bs2_sb[0:1, n * 512:(n + 1) * 512]:
                                        e.matmul(t, ones_bf[0:1, :], r,
                                                 start=False, stop=True), (),
                                        inc="pe")
                            bk.ready = tok
                            tc = p2 * 4 + sub
                            av = acc[:, tc * 1024 + n * 512:tc * 1024 + (n + 1) * 512]
                            tev = em.op("dve", lambda e, o=av, i=bk.ap[:]:
                                        e.tensor_tensor(o, o, i, ALU.add),
                                        [tok], inc="dve")
                            bk.free = tev

            if "L0" in layers and ph >= 4:
                we1v = we1.rearrange("e (k p) (j x n) -> e p j x k n",
                                     p=128, j=2, n=128)
                we2v = we2.rearrange("e (f p) n -> e p f n", p=128)
                exp_barrier = em.snap()
                e1bufs = (ar(A_E1A, 2048), ar(A_E1B, 2048))
                e1free = [exp_barrier, exp_barrier]
                e2free = exp_barrier
                for ex in range(NE0):
                    t_be2 = em.dma("gp", be2row[:], be2_sb[ex:ex + 1, :],
                                   [("dve", em.cnt["dve"])], sem="be2")
                    for gg in range(3):
                        t_we2 = em.dma("gp",
                                       ar(A_E2 + gg * 8192, 8192)
                                       .rearrange("p (f n) -> p f n", n=1024),
                                       we2v[ex, :, 8 * gg:8 * gg + 8, :],
                                       [e2free], sem="e2a")
                    for f in range(FE):
                        buf = e1bufs[f % 2]
                        t_w = em.dma("gp",
                                     buf.rearrange("p (j k n) -> p j k n", j=2, n=128),
                                     we1v[ex, :, :, f, :, :], [e1free[f % 2]],
                                     sem=("e1a", "e1b")[f % 2])
                        par = (f % 2) * 4
                        bksA = (P[par], P[par + 1])
                        bksG = (P[par + 2], P[par + 3])
                        for half, bks, jof in ((0, bksA, 0), (1, bksG, 1024)):
                            pe_inter(em, bks,
                                     (lambda k, b=buf, j=jof:
                                      b[:, j + k * 128:j + (k + 1) * 128]),
                                     (lambda k, tt: xn_sl(k, 128 + tt * 512, 512)),
                                     [t_w, t_xn])
                        for tt in range(2):
                            bkA, bkG = bksA[tt], bksG[tt]
                            gts = gt_scr[:, tt * 512:(tt + 1) * 512]
                            ats = at_scr[:, tt * 512:(tt + 1) * 512]
                            bc = ex * 48
                            tg = em.op("act", lambda e, o=gts, i=bkG.ap[:],
                                       b=be1_sb[:, bc + 24 + f:bc + 25 + f]:
                                       e.activation(o, i, AF.Gelu, bias=b, scale=1.0),
                                       [bkG.ready, gwar[tt]],
                                       inc="act")
                            bkG.free = tg
                            ta2 = em.op("act", lambda e, o=ats, i=bkA.ap[:],
                                        b=be1_sb[:, bc + f:bc + 1 + f]:
                                        e.activation(o, i, AF.Identity, bias=b, scale=1.0),
                                        [bkA.ready, gwar[tt]], inc="act")
                            bkA.free = ta2
                            tm = em.op("dve", lambda e,
                                       o=ar(A_ACT + f * 1024 + tt * 512, 512),
                                       a=ats, g=gts:
                                       e.tensor_tensor(o, a, g, ALU.mult),
                                       [tg, ta2], inc="dve")
                            gwar[tt] = tm
                        e1free[f % 2] = gwar[1]
                    t_act_all = ("dve", em.cnt["dve"])
                    t_we2_tok = ("e2a", em.cnt["e2a"])
                    for p2 in range(2):
                        for f in range(FE):
                            for sub in range(4):
                                for n in range(2):
                                    bk = P[sub * 2 + n]
                                    l = ar(A_ACT + f * 1024 + p2 * 512 + sub * 128, 128)
                                    r = ar(A_E2 + f * 1024 + n * 512, 512)
                                    w = ([t_we2_tok, t_act_all, bk.free]
                                         if f == 0 else ())
                                    em.op("pe", lambda e, t=bk.ap[:], l=l, r=r,
                                          st=(f == 0):
                                          e.matmul(t, l, r, start=st, stop=False),
                                          w)
                        for sub in range(4):
                            for n in range(2):
                                bk = P[sub * 2 + n]
                                tok = em.op("pe", lambda e, t=bk.ap[:],
                                            r=be2row[0:1, n * 512:(n + 1) * 512]:
                                            e.matmul(t, ones_bf[0:1, :], r,
                                                     start=False, stop=True),
                                            [t_be2], inc="pe")
                                bk.ready = tok
                                tc = p2 * 4 + sub
                                av = acc[:, tc * 1024 + n * 512:
                                         tc * 1024 + (n + 1) * 512]
                                wc = wtok[:, tc * NE0 + ex:tc * NE0 + ex + 1]
                                tev = em.op("dve", lambda e, o=av, i=bk.ap[:], s=wc:
                                            e.scalar_tensor_tensor(o, i, s, o,
                                                                   ALU.mult, ALU.add),
                                            [tok, t_wtok], inc="dve")
                                bk.free = tev
                        if p2 == 1:
                            e2free = ("pe", em.cnt["pe"])

            if "L1" in layers:
                l1_barrier = em.snap()
                chunks = ln_chunks(q, False)
                if "L0" in layers:
                    tok_srcs = [("dve", em.cnt["dve"])] * 8
                else:
                    tok_srcs = [t_main] * 8
                emit_ln(q, chunks, tok_srcs, update_acc=True)
                t_xn1 = ("dve", em.cnt["dve"])

                t_cb = em.dma("gp", ar(A_CB, 8192).rearrange("p (i j) -> p i j", j=1024),
                              cbind.rearrange("(i p) j -> p i j", p=128), l1_barrier, sem="cb")
                t_cc = em.dma("gp", ar(A_CC, 8192).rearrange("p (i j) -> p i j", j=1024),
                              ccorr.rearrange("(i p) j -> p i j", p=128), l1_barrier, sem="cc")
                for j in range(KC):
                    par = j % 2
                    bks = (P[par * 2], P[par * 2 + 1])
                    bscr = f32p[:, 2 * 1024:3 * 1024]
                    for tt in range(2):
                        bk = bks[tt]
                        items = []
                        for iC in range(KC):
                            items.append((bk.ap[:],
                                          ar(A_CB + iC * 1024 + j * 128, 128),
                                          xn_sl(iC, 128 + tt * 512, 512)))
                        tok = pe_group(em, bk, items, [t_cb, t_xn1])
                        tev = em.op("dve", lambda e, o=bscr[:, tt * 512:(tt + 1) * 512],
                                    i=bk.ap[:]: e.tensor_copy(o, i), [tok],
                                    inc="dve")
                        bk.free = tev
                    dsp()
                    tsc = em.op("dve", lambda e, o=f32p[:, 3072:4096], d0=dtile[:], d1=bscr,
                                ini=carries[:, j:j + 1]:
                                e.tensor_tensor_scan(o, d0, d1, ini, ALU.mult,
                                                     ALU.add), [t_dtile], inc="dve")
                    dsp()
                    em.op("dve", lambda e, o=carries[:, j:j + 1],
                          i=f32p[:, 4095:4096]: e.tensor_copy(o, i))
                    em.op("dve", lambda e, o=ar(A_HOLO + j * 1024, 1024), i=f32p[:, 3072:4096]:
                          e.tensor_copy(o, i), inc="dve")
                t_holo = ("dve", em.cnt["dve"])
                for tc in range(8):
                    par = tc % 2
                    bks = (P[4 + par * 2], P[5 + par * 2])
                    for n in range(2):
                        bk = bks[n]
                        items = []
                        for iC in range(KC):
                            items.append((bk.ap[:],
                                          ar(A_HOLO + iC * 1024 + tc * 128, 128),
                                          ar(A_CC + iC * 1024 + n * 512, 512)))
                        tok = pe_group(em, bk, items, [t_cc, t_holo])
                        av = acc[:, tc * 1024 + n * 512:tc * 1024 + (n + 1) * 512]
                        tev = em.op("dve", lambda e, o=av, i=bk.ap[:], s=grep[:]:
                                    e.scalar_tensor_tensor(o, i, s, o, ALU.mult,
                                                           ALU.add),
                                    [tok, t_grep], inc="dve")
                        bk.free = tev

                prev_tex1 = None
                for tc in range(8):
                    bk = P[tc % 2]
                    items = []
                    for k in range(KC):
                        items.append((bk.ap[:, 0:16],
                                      xn_sl(k, 128 + tc * 128, 128),
                                      rwt_sb[:, k * 16:(k + 1) * 16]))
                    tok = pe_group(em, bk, items, [t_rwt, t_xn1])
                    trr = em.op("dve", lambda e, o=r_scr[:], i=bk.ap[:, 0:16]:
                                e.tensor_copy(o, i), [tok, prev_tex1], inc="dve")
                    bk.free = trr
                    dsp()
                    tmx = em.op("dve", lambda e, o=wstat[:, 0:1], i=r_scr[:]:
                                e.reduce_max(o, i, AX.X, negate=True), inc="dve")
                    tex = em.op("act", lambda e, o=r_scr[:], i=r_scr[:],
                                b=wstat[:, 0:1]:
                                e.activation(o, i, AF.Exp, bias=b, scale=1.0),
                                [tmx], inc="act")
                    prev_tex1 = tex
                    em.op("dve", lambda e, o=wstat[:, 1:2], i=r_scr[:]:
                          e.reduce_sum(o, i, AX.X), [tex])
                    dsp()
                    em.op("dve", lambda e, o=wstat[:, 2:3], i=wstat[:, 1:2]:
                          e.reciprocal(o, i))
                    dsp()
                    em.op("dve", lambda e, o=r_scr[:], i=r_scr[:], s=wstat[:, 2:3]:
                          e.tensor_scalar(o, i, s, None, ALU.mult))
                    dsp()
                    em.op("dve", lambda e, o=m8[:], i=r_scr[:]: e.max(o, i))
                    dsp()
                    em.op("dve", lambda e, o=eqA[:], i=r_scr[:], t4=m8[:, 3:4]:
                          e.tensor_scalar(o, i, t4, None, ALU.is_ge))
                    dsp()
                    em.op("dve", lambda e, o=gates[:, tc * 16:(tc + 1) * 16],
                          a=r_scr[:], b=eqA[:]:
                          e.tensor_tensor(o, a, b, ALU.mult), inc="dve")
                t_gates = ("dve", em.cnt["dve"])

                supv = supt.rearrange("s (k p) e -> s p k e", p=128)
                sdnv = sdnt.rearrange("s (f p) n -> s p f n", p=128)
                rupv = rupt.rearrange("s (k p) e -> s p k e", p=128)
                rdnv = rdnt.rearrange("s (f p) n -> s p f n", p=128)
                upbufs = (ar(A_SUPA, 4096), ar(A_SUPB, 4096))
                dnbufs = (ar(A_SDNA, 4096), ar(A_SDNB, 4096))
                upfree = [None, None]
                dnfree = [None, None]
                ffn1_barrier = em.snap()
                for xi in range(18):
                    par = xi % 2
                    if xi < 2:
                        upsrc, dnsrc, gcol = supv[xi], sdnv[xi], None
                    else:
                        r = xi - 2
                        upsrc, dnsrc, gcol = rupv[r], rdnv[r], r
                    t_up = em.dma("gp",
                                  upbufs[par].rearrange("p (k e) -> p k e", e=512),
                                  upsrc, [upfree[par]] + (ffn1_barrier if xi < 2 else []),
                                  sem=("ua", "ub")[par])
                    t_dn = em.dma("gp",
                                  dnbufs[par].rearrange("p (f n) -> p f n", n=1024),
                                  dnsrc, [dnfree[par]] + (ffn1_barrier if xi < 2 else []),
                                  sem=("da", "db")[par])
                    tok_g = {}
                    for ec in range(4):
                        for tt in range(2):
                            bk = P[(ec % 2) * 2 + tt]
                            items = []
                            for k in range(KC):
                                items.append((bk.ap[:],
                                              upbufs[par][:, k * 512 + ec * 128:
                                                          k * 512 + (ec + 1) * 128],
                                              xn_sl(k, 128 + tt * 512, 512)))
                            tok = pe_group(em, bk, items, [t_up, t_xn1])
                            tg = em.op("act", lambda e,
                                       o=ar(A_ACTR + ec * 1024 + tt * 512, 512),
                                       i=bk.ap[:]:
                                       e.activation(o, i, AF.Gelu, bias=0.0,
                                                    scale=1.0), [tok], inc="act")
                            bk.free = tg
                            tok_g[(ec, tt)] = tg
                    upfree[par] = ("act", em.cnt["act"])
                    t_actr = ("act", em.cnt["act"])
                    for p2 in range(2):
                        for sub in range(4):
                            for n in range(2):
                                bk = P[sub * 2 + n]
                                items = []
                                for ec in range(4):
                                    items.append((bk.ap[:],
                                                  ar(A_ACTR + ec * 1024 + p2 * 512
                                                     + sub * 128, 128),
                                                  dnbufs[par][:, ec * 1024 + n * 512:
                                                              ec * 1024 + (n + 1) * 512]))
                                tok = pe_group(em, bk, items,
                                               [t_dn, t_actr])
                                tc = p2 * 4 + sub
                                av = acc[:, tc * 1024 + n * 512:
                                         tc * 1024 + (n + 1) * 512]
                                if gcol is None:
                                    tev = em.op("dve", lambda e, o=av, i=bk.ap[:]:
                                                e.tensor_tensor(o, o, i, ALU.add),
                                                [tok], inc="dve")
                                else:
                                    gc = gates[:, tc * 16 + gcol:tc * 16 + gcol + 1]
                                    tev = em.op("dve", lambda e, o=av, i=bk.ap[:],
                                                s=gc:
                                                e.scalar_tensor_tensor(
                                                    o, i, s, o, ALU.mult, ALU.add),
                                                [tok, t_gates], inc="dve")
                                bk.free = tev
                    dnfree[par] = ("dve", em.cnt["dve"])

            store_barrier = em.snap()
            em.dma("sp", dst[qs:qs + QT, :].rearrange("(c p) n -> p c n", p=128),
                   acc[:].rearrange("p (c n) -> p c n", n=1024), store_barrier, sem="st")

        if dump:
            dump_o = nc.dram_tensor("dump_arena", [128, 57344], BF16,
                                    kind="ExternalOutput").ap()
            dump_x = nc.dram_tensor("dump_xn", [128, KC * XNW], BF16,
                                    kind="ExternalOutput").ap()
            em.limit = None
            snapf = em.snap()
            em.dma("sp", dump_o, arena[:], snapf, sem="st")
            em.dma("sp", dump_x, xn[:], snapf, sem="st")
        for s in ("dio", "sx", "st"):
            em.final_wait("sp", s)
        for s in ("dwm", "wW", "s1a", "s1b", "w2s", "e1a", "e1b", "e2a",
                  "e2b", "be2", "cb", "cc", "ua", "ub", "da", "db"):
            em.final_wait("gp", s)

        @block.sync
        def _(eng):
            for f in em.ops["sp"]:
                f(eng, sems)

        @block.gpsimd
        def _(eng):
            for f in em.ops["gp"]:
                f(eng, sems)

        @block.tensor
        def _(eng):
            for f in em.ops["pe"]:
                f(eng, sems)

        @block.scalar
        def _(eng):
            for f in em.ops["act"]:
                f(eng, sems)

        @block.vector
        def _(eng):
            for f in em.ops["dve"]:
                f(eng, sems)

    return nc


def _host_prep(inputs):
    g = {k: np.asarray(v, dtype=np.float32) for k, v in inputs.items()}
    idxf = np.arange(D)
    roll_conv = (idxf[None, :] - idxf[:, None]) % D
    roll_corr = (idxf[:, None] - idxf[None, :]) % D
    c = lambda a: np.ascontiguousarray(a, dtype=np.float32)
    common = dict(
        wq=c(g["L0_attn_Wq"]), wk=c(g["L0_attn_Wk"]),
        wv=c(g["L0_attn_Wv"]), wo=c(g["L0_attn_Wo"]),
        ws1=c(g["L0_ffn_Ws1"]), ws2=c(g["L0_ffn_Ws2"]),
        we1=c(g["L0_ffn_We1"]), we2=c(g["L0_ffn_We2"]),
        wr=c(g["L0_ffn_Wr"]),
        bq_c=c(g["L0_attn_bq"].reshape(8, 128).T),
        bk_c=c(g["L0_attn_bk"].reshape(8, 128).T),
        bs1_c=c(g["L0_ffn_bs1"].reshape(32, 128).T),
        be1_c=c(g["L0_ffn_be1"].reshape(NE0, 48, 128).transpose(2, 0, 1)
                .reshape(128, NE0 * 48)),
        bv_r=c(g["L0_attn_bv"].reshape(1, D)),
        bo_r=c(g["L0_attn_bo"].reshape(1, D)),
        bs2_r=c(g["L0_ffn_bs2"].reshape(1, D)),
        br_r=c(g["L0_ffn_br"].reshape(1, NE0)),
        be2=c(g["L0_ffn_be2"]),
        cbind=c(g["L1_attn_bind_key"][roll_conv]),
        ccorr=c(g["L1_attn_unbind_key"][roll_corr]),
        supt=c(g["L1_ffn_shared_up"].transpose(0, 2, 1)),
        sdnt=c(g["L1_ffn_shared_down"].transpose(0, 2, 1)),
        rupt=c(g["L1_ffn_routed_up"].transpose(0, 2, 1)),
        rdnt=c(g["L1_ffn_routed_down"].transpose(0, 2, 1)),
        rwt=c(g["L1_ffn_router_w"].T),
        decay_rep=c(np.broadcast_to(g["L1_attn_decay"].reshape(1, 1), (128, 1))),
        gate_rep=c(np.broadcast_to(g["L1_attn_gate"].reshape(1, 1), (128, 1))),
    )
    x = c(g["L0_attn_x"])
    return common, x


def kernel(**inputs):
    common, x = _host_prep(inputs)
    ncores = 8
    if FUSED:
        nc = build_nc(("L0", "L1"))
        in_maps = [dict(common, src=x[b]) for b in range(ncores)]
        res = run_bass_kernel_spmd(nc, in_maps, core_ids=list(range(ncores)))
        out = np.stack([np.asarray(res.results[b]["dst"]) for b in range(ncores)])
    else:
        nc0 = build_nc(("L0",))
        in_maps = [dict(common, src=x[b]) for b in range(ncores)]
        r0 = run_bass_kernel_spmd(nc0, in_maps, core_ids=list(range(ncores)))
        mid = [np.asarray(r0.results[b]["dst"]) for b in range(ncores)]
        nc1 = build_nc(("L1",))
        in_maps = [dict(common, src=mid[b]) for b in range(ncores)]
        r1 = run_bass_kernel_spmd(nc1, in_maps, core_ids=list(range(ncores)))
        out = np.stack([np.asarray(r1.results[b]["dst"]) for b in range(ncores)])
    return out.astype(np.float32)


# version 03
# speedup vs baseline: 1.1324x; 1.1324x over previous
import contextlib
import numpy as np
import ml_dtypes
import concourse.bass as bass
import concourse.mybir as mybir
from concourse.bass_utils import run_bass_kernel_spmd

F32 = mybir.dt.float32
BF16 = mybir.dt.bfloat16
AF = mybir.ActivationFunctionType
ALU = mybir.AluOpType
AX = mybir.AxisListType

D = 1024
S = 4096
QT = 1024
NQ = S // QT
KC = D // 128
XNW = QT + 256
NE0 = 12
FE = 24
EPS = 1e-5

FUSED = True


class Em:
    ENGS = ("sp", "gp", "pe", "act", "dve")
    DMA_SEMS = ("dio", "dwm", "sx", "st", "wW", "s1a", "s1b", "w2s",
                "e1a", "e1b", "e2a", "e2b", "be2", "cb", "cc",
                "ua", "ub", "da", "db", "v2")
    SEMS = ("pe", "act", "dve") + DMA_SEMS

    def __init__(self, limit=None):
        self.ops = {e: [] for e in self.ENGS}
        self.cnt = dict.fromkeys(self.SEMS, 0)
        self.wm = {}
        self.seq = 0
        self.limit = limit

    def _filter(self, eng, waits):
        out = []
        stack = list(waits)
        while stack:
            t = stack.pop()
            if t is None:
                continue
            if isinstance(t, list):
                stack.extend(t)
                continue
            s, v = t
            if v <= self.wm.get((eng, s), 0):
                continue
            self.wm[(eng, s)] = v
            out.append((s, v))
        return out

    def op(self, eng, fn, waits=(), inc=None, n=1):
        self.seq += 1
        if self.limit is not None and self.seq > self.limit:
            return (inc, self.cnt[inc]) if inc else None
        ws = self._filter(eng, waits)
        tok = None
        if inc:
            self.cnt[inc] += n
            tok = (inc, self.cnt[inc])

        def replay(eo, sems, _ws=ws, _fn=fn, _inc=inc, _n=n):
            for s, v in _ws:
                eo.wait_ge(sems[s], v)
            ins = _fn(eo)
            if _inc:
                ins.then_inc(sems[_inc], _n)

        self.ops[eng].append(replay)
        return tok

    def dma(self, eng, out_ap, in_ap, waits=(), sem=None):
        if sem is None:
            sem = "dwm" if eng == "gp" else "dio"
        return self.op(eng, lambda e, o=out_ap, i=in_ap: e.dma_start(out=o, in_=i),
                       waits, inc=sem, n=16)

    def snap(self):
        return [(s, v) for s, v in self.cnt.items() if v > 0]

    def barrier_all(self, engs=None, always=False):
        snap = self.snap()
        for eng in (engs or self.ENGS):
            ws = self._filter(eng, snap)
            if ws:
                def rp(eo, sems, _ws=ws):
                    for s, v in _ws:
                        eo.wait_ge(sems[s], v)
                self.ops[eng].append(rp)

    def final_wait(self, eng, sem):
        v = self.cnt[sem]
        fw = lambda eo, sems, s=sem, vv=v: eo.wait_ge(sems[s], vv)
        self.ops[eng].append(fw)


class Bank:
    def __init__(self, ap):
        self.ap = ap
        self.free = None
        self.ready = None


def pe_group(em, bank, items, waits=(), ready=True):
    n = len(items)
    tok = None
    for i, (tgt, l, r) in enumerate(items):
        w = list(waits) + [bank.free] if i == 0 else ()
        tok = em.op(
            "pe",
            (lambda e, t=tgt, l=l, r=r, st=(i == 0), sp=(i == n - 1):
             e.matmul(t, l, r, start=st, stop=sp)),
            w, inc="pe" if i == n - 1 else None)
    if ready:
        bank.ready = tok
    return tok


def pe_inter(em, banks, lhs_fn, rhs_fn, waits=(), nk=8):
    for k in range(nk):
        for i, bk in enumerate(banks):
            w = (list(waits) + [bk.free]) if k == 0 else ()
            tok = em.op(
                "pe",
                (lambda e, t=bk.ap[:], l=lhs_fn(k), r=rhs_fn(k, i),
                 st=(k == 0), sp=(k == nk - 1):
                 e.matmul(t, l, r, start=st, stop=sp)),
                w, inc="pe" if k == nk - 1 else None)
            if k == nk - 1:
                bk.ready = tok


def ttiles(t0, t1, w=512):
    out = []
    t = t0
    while t < t1:
        out.append((t, min(w, t1 - t)))
        t += w
    return out


def build_nc(layers=("L0", "L1"), nquarters=NQ, detect_races=True, ph=99, trunc=None,
             dump=False):
    nc = bass.Bass("TRN2", target_bir_lowering=False,
                   detect_race_conditions=detect_races)
    dt_in = {}

    def din(name, shape, dt=F32):
        dt_in[name] = nc.dram_tensor(name, list(shape), dt, kind="ExternalInput").ap()
        return dt_in[name]

    src = din("src", (S, D))
    dst = nc.dram_tensor("dst", [S, D], F32, kind="ExternalOutput").ap()

    wq = din("wq", (D, D)); wk = din("wk", (D, D)); wv = din("wv", (D, D)); wo = din("wo", (D, D))
    ws1 = din("ws1", (D, 4096)); ws2 = din("ws2", (2048, D))
    we1 = din("we1", (NE0, D, 6144)); we2 = din("we2", (NE0, 3072, D))
    wr = din("wr", (D, NE0))
    bq_c = din("bq_c", (128, 8)); bk_c = din("bk_c", (128, 8))
    bs1_c = din("bs1_c", (128, 32)); be1_c = din("be1_c", (128, 576))
    bv_r = din("bv_r", (1, D)); bo_r = din("bo_r", (1, D)); bs2_r = din("bs2_r", (1, D))
    br_r = din("br_r", (1, NE0)); be2_d = din("be2", (NE0, D))
    cbind = din("cbind", (D, D)); ccorr = din("ccorr", (D, D))
    supt = din("supt", (2, D, 512)); sdnt = din("sdnt", (2, 512, D))
    rupt = din("rupt", (16, D, 512)); rdnt = din("rdnt", (16, 512, D))
    rwt = din("rwt", (D, 16))
    decay_rep = din("decay_rep", (128, 1)); gate_rep = din("gate_rep", (128, 1))

    ident_d = nc.inline_tensor(np.eye(128, dtype=np.float32), name="ident")
    counts = np.ones(S, np.float32) * 2.0
    counts[:64] = 1.0
    counts[-64:] = 1.0
    rcw_np = np.zeros((128, 63), np.float32)
    for w in range(63):
        rcw_np[:, w] = 1.0 / counts[64 * w:64 * w + 128]
    rcw_d = nc.inline_tensor(rcw_np, name="rcw")
    r64_np = np.roll(np.eye(128, dtype=np.float32), 64, axis=1)
    r64f_d = nc.inline_tensor(r64_np, name="r64f")

    em = Em(limit=trunc)

    with contextlib.ExitStack() as ctx:
        E = ctx.enter_context

        acc = E(nc.sbuf_tensor("acc", [128, 8 * 1024], F32))
        xn = E(nc.sbuf_tensor("xn", [128, KC * XNW], BF16))
        arena = E(nc.sbuf_tensor("arena", [128, 57344], BF16))
        f32p = E(nc.sbuf_tensor("f32p", [128, 4 * 1024], F32))
        dtile = E(nc.sbuf_tensor("dtile", [128, 1024], F32))
        stats = E(nc.sbuf_tensor("stats", [128, 64], F32))
        wstat = E(nc.sbuf_tensor("wstat", [128, 16], F32))
        wtok = E(nc.sbuf_tensor("wtok", [128, 96], F32))
        gates = E(nc.sbuf_tensor("gates", [128, 128], F32))
        r_scr = E(nc.sbuf_tensor("r_scr", [128, 16], F32))
        m8 = E(nc.sbuf_tensor("m8", [128, 8], F32))
        eqA = E(nc.sbuf_tensor("eqA", [128, 16], F32))
        eqB = E(nc.sbuf_tensor("eqB", [128, 16], F32))
        carries = E(nc.sbuf_tensor("carries", [128, 8], F32))
        dcol = E(nc.sbuf_tensor("dcol", [128, 2], F32))
        epsc = E(nc.sbuf_tensor("epsc", [128, 1], F32))
        p32 = E(nc.sbuf_tensor("p32", [128, 2 * 128], F32))
        pT_sb = E(nc.sbuf_tensor("pT_sb", [128, 2 * 128], BF16))
        gt_scr = E(nc.sbuf_tensor("gt_scr", [128, 2 * 512], BF16))
        at_scr = E(nc.sbuf_tensor("at_scr", [128, 2 * 512], BF16))
        bqk_sb = E(nc.sbuf_tensor("bqk_sb", [128, 16], F32))
        bs1_sb = E(nc.sbuf_tensor("bs1_sb", [128, 32], F32))
        be1_sb = E(nc.sbuf_tensor("be1_sb", [128, 576], F32))
        rcw_sb = E(nc.sbuf_tensor("rcw_sb", [128, 63], F32))
        ident_sb = E(nc.sbuf_tensor("ident_sb", [128, 128], F32))
        r64f_sb = E(nc.sbuf_tensor("r64f_sb", [128, 128], F32))
        cp32 = E(nc.sbuf_tensor("cp32", [128, 2 * 128], F32))
        wr_sb = E(nc.sbuf_tensor("wr_sb", [128, 96], BF16))
        rwt_sb = E(nc.sbuf_tensor("rwt_sb", [128, 128], BF16))
        drep = E(nc.sbuf_tensor("drep", [128, 1], F32))
        grep = E(nc.sbuf_tensor("grep", [128, 1], F32))
        ones_bf = E(nc.sbuf_tensor("ones_bf", [1, 128], BF16))
        bv_sb = E(nc.sbuf_tensor("bv_sb", [1, D], BF16))
        bo_sb = E(nc.sbuf_tensor("bo_sb", [1, D], BF16))
        bs2_sb = E(nc.sbuf_tensor("bs2_sb", [1, D], BF16))
        br_sb = E(nc.sbuf_tensor("br_sb", [1, NE0], BF16))
        be2_sb = E(nc.sbuf_tensor("be2_sb", [NE0, D], BF16))
        be2row = E(nc.sbuf_tensor("be2row", [1, D], BF16))

        P = [Bank(E(nc.psum_tensor(f"P{i}", [128, 512], F32))) for i in range(8)]
        spacer = E(nc.sbuf_tensor("spacer", [128, 2], F32))

        def dsp():
            em.op("dve", lambda e: e.tensor_scalar(spacer[:, 0:1], spacer[:, 0:1],
                                                   1.0, None, ALU.mult))

        def asp():
            em.op("act", lambda e: e.activation(spacer[:, 1:2], spacer[:, 1:2],
                                                AF.Identity, bias=0.0, scale=1.0))

        sems = {}
        for sname in Em.SEMS:
            sems[sname] = E(nc.semaphore(sname))
        block = E(nc.Block())

        def ar(c0, w):
            return arena[:, c0:c0 + w]
        A_QT, A_KT, A_VT, A_V2 = 0, 10240, 20480, 30720
        A_OT, A_WA = 39936, 48128
        A_ACT = 0
        A_WS2 = 16384
        A_E2 = 24576
        A_E1A, A_E1B = 49152, 53248
        A_CB, A_CC, A_HOLO = 0, 8192, 16384
        A_ACTR = 0
        A_SUPA, A_SUPB = 4096, 8192
        A_SDNA, A_SDNB = 12288, 16384 + 4096

        def xn_sl(k, t0, w):
            return xn[:, k * XNW + t0:k * XNW + t0 + w]

        def ln_chunks(q, with_halo):
            lo = 1 if (q > 0 and with_halo) else 0
            hi = 1 if (q < 3 and with_halo) else 0
            ch = []
            if lo:
                ch.append((f32p[:, 0:1024], 0))
            for c in range(8):
                ch.append((acc[:, c * 1024:(c + 1) * 1024], 128 + c * 128))
            if hi:
                ch.append((f32p[:, 1024:2048], 1152))
            return ch

        def emit_ln(q, chunks, tok_srcs, update_acc):
            last = None
            tp_done = {}
            for idx, (src, t_loc) in enumerate(chunks):
                sc = stats[:, idx * 6:idx * 6 + 6]
                par = idx % 2
                tsrc = tok_srcs[idx] if idx < len(tok_srcs) else None
                xtsq = f32p[:, 2 * 1024:3 * 1024]
                em.op("dve", lambda e, o=xtsq, i=src:
                      e.tensor_tensor(o, i, i, ALU.mult),
                      [tsrc, tp_done.get(0)])
                em.op("dve", lambda e, o=sc[:, 0:1], i=src: e.reduce_sum(o, i, AX.X))
                em.op("dve", lambda e, o=sc[:, 2:3], i=xtsq:
                      e.reduce_sum(o, i, AX.X))
                em.op("dve", lambda e, o=sc[:, 1:2], i=sc[:, 0:1]:
                      e.tensor_scalar(o, i, -1.0 / D, None, ALU.mult))
                em.op("dve", lambda e, o=sc[:, 4:5], i=sc[:, 2:3]:
                      e.tensor_scalar(o, i, 1.0 / D, None, ALU.mult))
                em.op("dve", lambda e, o=sc[:, 3:4], i=sc[:, 1:2]:
                      e.tensor_scalar(o, i, i, None, ALU.mult))
                dsp()
                tvar = em.op("dve", lambda e, o=sc[:, 3:4], i=sc[:, 4:5], m=sc[:, 3:4]:
                             e.tensor_scalar(o, i, m, None, ALU.subtract),
                             (), inc="dve")
                em.op("act", lambda e, o=sc[:, 3:4], i=sc[:, 3:4]:
                      e.activation(o, i, AF.Ln, bias=epsc[:], scale=1.0), [tvar])
                asp()
                trstd = em.op("act", lambda e, o=sc[:, 3:4], i=sc[:, 3:4]:
                              e.activation(o, i, AF.Exp, bias=0.0, scale=-0.5),
                              inc="act")
                xt = f32p[:, 2 * 1024:3 * 1024]
                txt = em.op("dve", lambda e, o=xt, i=src, m=sc[:, 1:2], r=sc[:, 3:4]:
                            e.tensor_scalar(o, i, m, r, ALU.add, ALU.mult),
                            [trstd, tp_done.get(0)], inc="dve")
                ttp = None
                for j in range(KC):
                    bk = P[4 + (j % 4)]
                    sub = bk.ap[:, 0:128]
                    ttp = em.op("pe", lambda e, o=sub, i=xt[:, j * 128:(j + 1) * 128]:
                                e.transpose(o, i, ident_sb[:]), [txt, bk.free],
                                inc="pe")
                    bk.ready = ttp
                    tev = em.op("dve", lambda e, o=xn_sl(j, t_loc, 128), i=sub:
                                e.tensor_copy(o, i), [ttp], inc="dve")
                    bk.free = tev
                    last = tev
                tp_done[0] = ttp
                if update_acc:
                    em.op("dve", lambda e, o=sc[:, 0:1], i=sc[:, 3:4]:
                          e.tensor_scalar(o, i, 1.0, None, ALU.add))
                    em.op("dve", lambda e, o=sc[:, 1:2], i=sc[:, 1:2], r=sc[:, 3:4]:
                          e.tensor_scalar(o, i, r, None, ALU.mult))
                    dsp()
                    last = em.op("dve", lambda e, o=src, i=src, a=sc[:, 0:1],
                                 b=sc[:, 1:2]:
                                 e.tensor_scalar(o, i, a, b, ALU.mult, ALU.add),
                                 inc="dve")
            return last

        def down_proj(name, nf, act_off, wb_ap, tok_w, evac, extra_waits=(),
                      group_starts=None):
            for p2 in range(2):
                for f in range(nf):
                    for sub in range(4):
                        for n in range(2):
                            bk = P[sub * 2 + n]
                            tgt = bk.ap[:]
                            l = ar(act_off + f * 1024 + p2 * 512 + sub * 128, 128)
                            r = wb_ap(f, n)
                            first = (f == 0)
                            w = (list(extra_waits) + [tok_w(f), bk.free]) if first else \
                                ([tok_w(f)] if (group_starts and f in group_starts) else ())
                            em.op("pe", lambda e, t=tgt, l=l, r=r, st=first:
                                  e.matmul(t, l, r, start=st, stop=False), w)
                for sub in range(4):
                    for n in range(2):
                        bk = P[sub * 2 + n]
                        yield ("close", p2, sub, n, bk)

        em.op("dve", lambda e: e.memset(spacer[:], 0.0))
        em.op("dve", lambda e: e.memset(ones_bf[:], 1.0))
        em.op("dve", lambda e: e.memset(epsc[:], EPS))
        em.op("dve", lambda e: e.memset(xn[:], 0.0))
        em.op("dve", lambda e: e.memset(arena[:], 0.0))
        em.op("dve", lambda e: e.memset(carries[:], 0.0), inc="dve")
        t_ident = em.dma("sp", ident_sb[:], ident_d.ap())
        em.dma("sp", r64f_sb[:], r64f_d.ap())
        t_rcw = em.dma("sp", rcw_sb[:], rcw_d.ap())
        em.dma("sp", bqk_sb[:, 0:8], bq_c)
        em.dma("sp", bqk_sb[:, 8:16], bk_c)
        em.dma("sp", bs1_sb[:], bs1_c)
        em.dma("sp", be1_sb[:], be1_c)
        t_drep = em.dma("sp", drep[:], decay_rep)
        t_grep = em.dma("sp", grep[:], gate_rep)
        em.dma("gp", bv_sb[:], bv_r)
        em.dma("gp", bo_sb[:], bo_r)
        em.dma("gp", bs2_sb[:], bs2_r)
        em.dma("gp", br_sb[:], br_r)
        em.dma("gp", be2_sb[:], be2_d)
        em.dma("gp", wr_sb[:].rearrange("p (k e) -> p k e", e=NE0),
               wr.rearrange("(k p) e -> p k e", p=128))
        t_rwt = em.dma("gp", rwt_sb[:].rearrange("p (k e) -> p k e", e=16),
                       rwt.rearrange("(k p) e -> p k e", p=128))
        ta = em.op("act", lambda e: e.activation(dcol[:, 0:1], drep[:], AF.Exp,
                                                 bias=0.0, scale=-1.0),
                   [("dio", em.cnt["dio"])], inc="act")
        tb = em.op("dve", lambda e: e.tensor_scalar(dcol[:, 0:1], dcol[:, 0:1], 1.0,
                                                    None, ALU.add), [ta])
        dsp()
        tc_ = em.op("dve", lambda e: e.reciprocal(dcol[:, 1:2], dcol[:, 0:1]))
        em.op("dve", lambda e: e.memset(dtile[:], 1.0))
        dsp()
        t_dtile = em.op("dve", lambda e: e.tensor_scalar(dtile[:], dtile[:],
                                                         dcol[:, 1:2], None, ALU.mult),
                        inc="dve")

        for q in range(nquarters):
            qs = q * QT
            lo = 1 if q > 0 else 0
            hi = 1 if q < 3 else 0
            frame0 = qs - 128
            em.barrier_all()
            barrier = em.snap()

            em.dma("sp",
                   acc[:].rearrange("p (c n) -> p c n", n=1024),
                   src[qs:qs + QT, :].rearrange("(c p) n -> p c n", p=128),
                   barrier, sem="sx")
            if "L0" in layers:
                if lo:
                    em.dma("sp", f32p[:, 0:1024],
                           src[qs - 128:qs, :], barrier, sem="sx")
                if hi:
                    em.dma("sp", f32p[:, 1024:2048],
                           src[qs + QT:qs + QT + 128, :], barrier, sem="sx")
            t_main = t_hx0 = t_hx1 = ("sx", em.cnt["sx"])

            if "L0" in layers:
                chunks = ln_chunks(q, True)
                toks = []
                for (srcap, t_loc) in chunks:
                    if t_loc == 0:
                        toks.append(t_hx0)
                    elif t_loc == 1152:
                        toks.append(t_hx1)
                    else:
                        toks.append(t_main)
                t_xn = emit_ln(q, chunks, toks, update_acc=False)

                tmin = 0 if lo else 128
                tmax = 1280 if hi else 1152
                t_wtok = ("dve", em.cnt["dve"])

            if "L0" in layers and ph >= 1:
                t_wq = em.dma("gp", ar(A_WA, 8192).rearrange("p (k m) -> p k m", m=1024),
                              wq.rearrange("(k p) m -> p k m", p=128), [t_ident], sem="wW")
                tiles = ttiles(tmin, tmax)

                def proj_fm(dst_off, t_w, bias_base, waits_extra):
                    tok_mm_last = None
                    for m in range(KC):
                        bks = [P[(m % 2) * 3 + i] for i in range(len(tiles))]
                        for ti, (t0, wdt) in enumerate(tiles):
                            bk = bks[ti]
                            items = []
                            for k in range(KC):
                                items.append((bk.ap[:, 0:wdt],
                                              ar(A_WA + k * 1024 + m * 128, 128),
                                              xn_sl(k, t0, wdt)))
                            tok = pe_group(em, bk, items,
                                           [t_w, t_xn] + list(waits_extra))
                            tok_mm_last = tok
                            tev = em.op("act",
                                        lambda e, o=ar(dst_off + m * XNW + t0, wdt),
                                        i=bk.ap[:, 0:wdt],
                                        b=bqk_sb[:, bias_base + m:bias_base + m + 1]:
                                        e.activation(o, i, AF.Identity, bias=b, scale=1.0),
                                        [tok], inc="act")
                            bk.free = tev
                    return tok_mm_last

                tql = proj_fm(A_QT, t_wq, 0, [])
                t_wk = em.dma("gp", ar(A_WA, 8192).rearrange("p (k m) -> p k m", m=1024),
                              wk.rearrange("(k p) m -> p k m", p=128), [("pe", tql[1])], sem="wW")
                tkl = proj_fm(A_KT, t_wk, 8, [])
                tqk = (("act", em.cnt["act"]))

                t_wv = em.dma("gp", ar(A_WA, 8192).rearrange("p (k m) -> p k m", m=1024),
                              wv.rearrange("(k p) m -> p k m", p=128), [("pe", tkl[1])], sem="wW")
                vc0, vc1 = tmin // 128, tmax // 128
                for vc in range(vc0, vc1):
                    par = vc % 2
                    bks = (P[par * 2], P[par * 2 + 1])
                    seq = {0: [], 1: []}
                    for k in range(KC):
                        for n in range(2):
                            seq[n].append((bks[n].ap[:],
                                           xn_sl(k, vc * 128, 128),
                                           ar(A_WA + k * 1024 + n * 512, 512)))
                    for n in range(2):
                        bk = bks[n]
                        items = seq[n]
                        ntot = len(items) + 1
                        for i, (tgt, l, r) in enumerate(items):
                            w = [t_wv, t_xn, bk.free] if i == 0 else ()
                            em.op("pe", lambda e, t=tgt, l=l, r=r, st=(i == 0):
                                  e.matmul(t, l, r, start=st, stop=False), w)
                        tok = em.op("pe", lambda e, t=bk.ap[:],
                                    r=bv_sb[0:1, n * 512:(n + 1) * 512]:
                                    e.matmul(t, ones_bf[0:1, :], r, start=False,
                                             stop=True), (), inc="pe")
                        bk.ready = tok
                        tev = em.op("dve", lambda e,
                                    o=ar(A_VT + vc * 1024 + n * 512, 512),
                                    i=bk.ap[:]: e.tensor_copy(o, i), [tok], inc="dve")
                        bk.free = tev
                t_v_all = ("dve", em.cnt["dve"])
                em.dma("gp", ar(A_V2, 9216)[0:64, :], ar(A_VT, 9216)[64:128, :],
                       [t_v_all], sem="v2")
                em.dma("gp", ar(A_V2, 9216)[64:128, :],
                       ar(A_VT + 1024, 9216)[0:64, :], [t_v_all], sem="v2")
                t_v2 = ("v2", em.cnt["v2"])

                t_ot0 = em.op("dve", lambda e: e.memset(ar(A_OT, 8192), 0.0),
                              [tqk], inc="dve")
                s_first = max(0, qs - 64)
                s_last = min(S - 128, qs + QT - 64)
                t_scores_last = None
                pf_war = [None, None]
                ws_war = [None, None]
                for wi, s in enumerate(range(s_first, s_last + 1, 64)):
                    par = wi % 2
                    t_l = s - frame0
                    w_glob = s // 64
                    off = t_l % 128
                    vc = t_l // 128
                    bkS, bkT = P[4 + par], P[6 + par]
                    items = []
                    for k in range(KC):
                        items.append((bkS.ap[:, 0:128],
                                      ar(A_QT + k * XNW + t_l, 128),
                                      ar(A_KT + k * XNW + t_l, 128)))
                    tok_s = pe_group(em, bkS, items, [tqk])
                    t_scores_last = tok_s
                    ws = wstat[:, par * 8:par * 8 + 8]
                    tmx = em.op("dve", lambda e, o=ws[:, 0:1], i=bkS.ap[:, 0:128]:
                                e.reduce_max(o, i, AX.X, negate=True),
                                [tok_s, ws_war[par]], inc="dve")
                    dsp()
                    em.op("dve", lambda e, o=ws[:, 1:2], i=ws[:, 0:1]:
                          e.tensor_scalar(o, i, 0.125, None, ALU.mult), inc="dve")
                    tdv = ("dve", em.cnt["dve"])
                    pf = p32[:, par * 128:(par + 1) * 128]
                    texp = em.op("act", lambda e, o=pf, i=bkS.ap[:, 0:128],
                                 b=ws[:, 1:2]:
                                 e.activation(o, i, AF.Exp, bias=b, scale=0.125),
                                 [tdv, t_rcw, pf_war[par]], inc="act")
                    bkS.free = texp
                    ws_war[par] = texp
                    em.op("dve", lambda e, o=ws[:, 2:3], i=pf:
                          e.reduce_sum(o, i, AX.X), [texp])
                    dsp()
                    em.op("dve", lambda e, o=ws[:, 3:4], i=ws[:, 2:3]:
                          e.reciprocal(o, i))
                    dsp()
                    em.op("dve", lambda e, o=ws[:, 3:4], i=ws[:, 3:4],
                          r=rcw_sb[:, w_glob:w_glob + 1]:
                          e.tensor_scalar(o, i, r, None, ALU.mult))
                    dsp()
                    tpn = em.op("dve", lambda e, o=pf, i=pf, r=ws[:, 3:4]:
                                e.tensor_scalar(o, i, r, None, ALU.mult), inc="dve")
                    idmat = ident_sb if off == 0 else r64f_sb
                    ttp = em.op("pe", lambda e, o=bkT.ap[:, 0:128], i=pf, m=idmat:
                                e.transpose(o, i, m[:]), [tpn, bkT.free],
                                inc="pe")
                    bkT.ready = ttp
                    pf_war[par] = ttp
                    pts = pT_sb[:, par * 128:(par + 1) * 128]
                    tpt = em.op("dve", lambda e, o=pts, i=bkT.ap[:, 0:128]:
                                e.tensor_copy(o, i), [ttp], inc="dve")
                    bkT.free = tpt
                    psO = (P[0 + 2 * par], P[1 + 2 * par])
                    for n in range(2):
                        bk = psO[n]
                        if off == 0:
                            vsl = ar(A_VT + vc * 1024 + n * 512, 512)
                        else:
                            vsl = ar(A_V2 + vc * 1024 + n * 512, 512)
                        items = [(bk.ap[:], pts, vsl)]
                        tok_o = pe_group(em, bk, items, [tpt, t_v_all, t_v2])
                        if off == 0:
                            c = (s - qs) // 128
                            ov = ar(A_OT + c * 1024 + n * 512, 512)
                            pv = bk.ap[:]
                            tsc = em.op("dve", lambda e, o=ov, i=pv:
                                        e.tensor_tensor(o, o, i, ALU.add),
                                        [tok_o, t_ot0], inc="dve")
                        else:
                            cA = (s - qs - 64) // 128
                            tsc = None
                            if cA >= 0:
                                ov = ar(A_OT + cA * 1024 + n * 512, 512)[64:128, :]
                                pv = bk.ap[:][64:128, :]
                                tsc = em.op("dve", lambda e, o=ov, i=pv:
                                            e.tensor_tensor(o, o, i, ALU.add),
                                            [tok_o, t_ot0],
                                            inc="dve")
                            if cA + 1 <= 7:
                                ov = ar(A_OT + (cA + 1) * 1024 + n * 512, 512)[0:64, :]
                                pv = bk.ap[:][0:64, :]
                                tsc = em.op("dve", lambda e, o=ov, i=pv:
                                            e.tensor_tensor(o, o, i, ALU.add),
                                            [tok_o, t_ot0],
                                            inc="dve")
                        bk.free = tsc
                t_scatter_all = ("dve", em.cnt["dve"])

                t_wo = em.dma("gp", ar(A_WA, 8192).rearrange("p (k m) -> p k m", m=1024),
                              wo.rearrange("(k p) m -> p k m", p=128),
                              [t_scores_last], sem="wW")
                cp_war = [None, None]
                for tc in range(8):
                    for k in range(KC):
                        bkt = P[4 + (k % 2)]
                        cps = cp32[:, (k % 2) * 128:(k % 2 + 1) * 128]
                        tcp = em.op("dve", lambda e, o=cps,
                                    i=ar(A_OT + tc * 1024 + k * 128, 128):
                                    e.tensor_copy(o, i),
                                    [t_scatter_all, cp_war[k % 2]], inc="dve")
                        ttp = em.op("pe", lambda e, o=bkt.ap[:, 0:128], i=cps:
                                    e.transpose(o, i, ident_sb[:]),
                                    [tcp, bkt.free], inc="pe")
                        cp_war[k % 2] = ttp
                        tev = em.op("dve", lambda e,
                                    o=ar(A_QT + k * 1024 + tc * 128, 128),
                                    i=bkt.ap[:, 0:128]:
                                    e.tensor_copy(o, i), [ttp], inc="dve")
                        bkt.free = tev
                    t_ofm = ("dve", em.cnt["dve"])
                    for n in range(2):
                        bk = P[(tc % 2) * 2 + n]
                        items = []
                        for k in range(KC):
                            items.append((bk.ap[:],
                                          ar(A_QT + k * 1024 + tc * 128, 128),
                                          ar(A_WA + k * 1024 + n * 512, 512)))
                        for i, (tgt, l, r) in enumerate(items):
                            w = [t_wo, t_ofm, bk.free] if i == 0 else ()
                            em.op("pe", lambda e, t=tgt, l=l, r=r, st=(i == 0):
                                  e.matmul(t, l, r, start=st, stop=False), w)
                        tok = em.op("pe", lambda e, t=bk.ap[:],
                                    r=bo_sb[0:1, n * 512:(n + 1) * 512]:
                                    e.matmul(t, ones_bf[0:1, :], r, start=False,
                                             stop=True), (), inc="pe")
                        bk.ready = tok
                        av = acc[:, tc * 1024 + n * 512:tc * 1024 + (n + 1) * 512]
                        tev = em.op("dve", lambda e, o=av, i=bk.ap[:]:
                                    e.tensor_tensor(o, o, i, ALU.add),
                                    [tok, t_main], inc="dve")
                        bk.free = tev

            if "L0" in layers and ph >= 2:
                prev_tex = None
                for tc in range(8):
                    bk = P[4 + tc % 2]
                    items = []
                    for k in range(KC):
                        items.append((bk.ap[:, 0:NE0],
                                      xn_sl(k, 128 + tc * 128, 128),
                                      wr_sb[:, k * NE0:(k + 1) * NE0]))
                    for i, (tgt, l, r) in enumerate(items):
                        w = [bk.free] if i == 0 else ()
                        em.op("pe", lambda e, t=tgt, l=l, r=r, st=(i == 0):
                              e.matmul(t, l, r, start=st, stop=False), w)
                    tok = em.op("pe", lambda e, t=bk.ap[:, 0:NE0],
                                r=br_sb[0:1, :]:
                                e.matmul(t, ones_bf[0:1, :], r, start=False,
                                         stop=True), (), inc="pe")
                    bk.ready = tok
                    trr = em.op("dve", lambda e, o=r_scr[:, 0:NE0], i=bk.ap[:, 0:NE0]:
                                e.tensor_copy(o, i), [tok, prev_tex], inc="dve")
                    bk.free = trr
                    dsp()
                    em.op("dve", lambda e, o=m8[:], i=r_scr[:, 0:NE0]:
                          e.max(o, i))
                    dsp()
                    tsub = em.op("dve", lambda e, o=wstat[:, 0:1], a=m8[:, 0:1],
                                 b=m8[:, 1:2]:
                                 e.tensor_scalar(o, a, b, None, ALU.subtract),
                                 inc="dve")
                    tex = em.op("act", lambda e, o=wstat[:, 1:2], i=wstat[:, 0:1]:
                                e.activation(o, i, AF.Exp, bias=0.0,
                                             scale=-1.0 / 1.5), [tsub], inc="act")
                    prev_tex = tex
                    em.op("dve", lambda e, o=wstat[:, 1:2], i=wstat[:, 1:2]:
                          e.tensor_scalar(o, i, 1.0, None, ALU.add), [tex])
                    dsp()
                    em.op("dve", lambda e, o=wstat[:, 2:3], i=wstat[:, 1:2]:
                          e.reciprocal(o, i))
                    dsp()
                    em.op("dve", lambda e, o=wstat[:, 3:4], i=wstat[:, 2:3]:
                          e.tensor_scalar(o, i, -1.0, 1.0, ALU.mult, ALU.add))
                    em.op("dve", lambda e, o=eqA[:, 0:NE0], i=r_scr[:, 0:NE0],
                          m=m8[:, 0:1]: e.tensor_scalar(o, i, m, None, ALU.is_equal))
                    dsp()
                    em.op("dve", lambda e, o=eqA[:, 0:NE0], i=eqA[:, 0:NE0],
                          s=wstat[:, 2:3]: e.tensor_scalar(o, i, s, None, ALU.mult))
                    em.op("dve", lambda e, o=eqB[:, 0:NE0], i=r_scr[:, 0:NE0],
                          m=m8[:, 1:2]: e.tensor_scalar(o, i, m, None, ALU.is_equal))
                    dsp()
                    em.op("dve", lambda e, o=wtok[:, tc * NE0:(tc + 1) * NE0],
                          i=eqB[:, 0:NE0], s=wstat[:, 3:4], a=eqA[:, 0:NE0]:
                          e.scalar_tensor_tensor(o, i, s, a, ALU.mult, ALU.add),
                          inc="dve")
                t_wtok = ("dve", em.cnt["dve"])

            if "L0" in layers and ph >= 3:
                ffn_barrier = em.snap()
                t_ws2 = em.dma("gp", ar(A_WS2, 16384).rearrange("p (f n) -> p f n", n=1024),
                               ws2.rearrange("(f p) n -> p f n", p=128), ffn_barrier, sem="w2s")
                ws1v = ws1.rearrange("(k p) (j x n) -> p j x k n", p=128, j=2, n=128)
                s1bufs = (ar(A_E1A, 2048), ar(A_E1B, 2048))
                s1free = [None, None]
                gwar = {0: None, 1: None}
                for f in range(16):
                    buf = s1bufs[f % 2]
                    t_w = em.dma("gp",
                                 buf.rearrange("p (j k n) -> p j k n", j=2, n=128),
                                 ws1v[:, :, f, :, :],
                                 [s1free[f % 2]] + (ffn_barrier if f < 2 else []),
                                 sem=("s1a", "s1b")[f % 2])
                    par = (f % 2) * 4
                    bksA = (P[par], P[par + 1])
                    bksG = (P[par + 2], P[par + 3])
                    for half, bks, jof in ((0, bksA, 0), (1, bksG, 1024)):
                        pe_inter(em, bks,
                                 (lambda k, b=buf, j=jof:
                                  b[:, j + k * 128:j + (k + 1) * 128]),
                                 (lambda k, tt: xn_sl(k, 128 + tt * 512, 512)),
                                 [t_w, t_xn])
                    for tt in range(2):
                        bkA, bkG = bksA[tt], bksG[tt]
                        gts = gt_scr[:, tt * 512:(tt + 1) * 512]
                        ats = at_scr[:, tt * 512:(tt + 1) * 512]
                        tg = em.op("act", lambda e, o=gts, i=bkG.ap[:],
                                   b=bs1_sb[:, 16 + f:17 + f]:
                                   e.activation(o, i, AF.Gelu, bias=b, scale=1.0),
                                   [bkG.ready, gwar[tt]], inc="act")
                        bkG.free = tg
                        ta2 = em.op("act", lambda e, o=ats, i=bkA.ap[:],
                                    b=bs1_sb[:, f:f + 1]:
                                    e.activation(o, i, AF.Identity, bias=b, scale=1.0),
                                    [bkA.ready, gwar[tt]], inc="act")
                        bkA.free = ta2
                        tm = em.op("dve", lambda e,
                                   o=ar(A_ACT + f * 1024 + tt * 512, 512),
                                   a=ats, g=gts:
                                   e.tensor_tensor(o, a, g, ALU.mult),
                                   [tg, ta2], inc="dve")
                        gwar[tt] = tm
                    s1free[f % 2] = gwar[1]
                t_act_all = ("dve", em.cnt["dve"])
                for p2 in range(2):
                    for f in range(16):
                        for sub in range(4):
                            for n in range(2):
                                bk = P[sub * 2 + n]
                                l = ar(A_ACT + f * 1024 + p2 * 512 + sub * 128, 128)
                                r = ar(A_WS2 + f * 1024 + n * 512, 512)
                                w = [t_ws2, t_act_all, bk.free] if f == 0 else ()
                                em.op("pe", lambda e, t=bk.ap[:], l=l, r=r,
                                      st=(f == 0):
                                      e.matmul(t, l, r, start=st, stop=False), w)
                    for sub in range(4):
                        for n in range(2):
                            bk = P[sub * 2 + n]
                            tok = em.op("pe", lambda e, t=bk.ap[:],
                                        r=bs2_sb[0:1, n * 512:(n + 1) * 512]:
                                        e.matmul(t, ones_bf[0:1, :], r,
                                                 start=False, stop=True), (),
                                        inc="pe")
                            bk.ready = tok
                            tc = p2 * 4 + sub
                            av = acc[:, tc * 1024 + n * 512:tc * 1024 + (n + 1) * 512]
                            tev = em.op("dve", lambda e, o=av, i=bk.ap[:]:
                                        e.tensor_tensor(o, o, i, ALU.add),
                                        [tok], inc="dve")
                            bk.free = tev

            if "L0" in layers and ph >= 4:
                we1v = we1.rearrange("e (k p) (j x n) -> e p j x k n",
                                     p=128, j=2, n=128)
                we2v = we2.rearrange("e (f p) n -> e p f n", p=128)
                exp_barrier = em.snap()
                e1bufs = (ar(A_E1A, 2048), ar(A_E1B, 2048))
                e1free = [exp_barrier, exp_barrier]
                e2free = exp_barrier
                for ex in range(NE0):
                    for f in range(FE):
                        buf = e1bufs[f % 2]
                        t_w = em.dma("gp",
                                     buf.rearrange("p (j k n) -> p j k n", j=2, n=128),
                                     we1v[ex, :, :, f, :, :], [e1free[f % 2]],
                                     sem=("e1a", "e1b")[f % 2])
                        if f in (2, 4, 6):
                            gg = f // 2 - 1
                            em.dma("gp",
                                   ar(A_E2 + gg * 8192, 8192)
                                   .rearrange("p (f n) -> p f n", n=1024),
                                   we2v[ex, :, 8 * gg:8 * gg + 8, :],
                                   [e2free], sem="e2a")
                        if f == 8:
                            t_be2 = em.dma("gp", be2row[:], be2_sb[ex:ex + 1, :],
                                           [("dve", em.cnt["dve"])], sem="be2")
                        par = (f % 2) * 4
                        bksA = (P[par], P[par + 1])
                        bksG = (P[par + 2], P[par + 3])
                        for half, bks, jof in ((0, bksA, 0), (1, bksG, 1024)):
                            pe_inter(em, bks,
                                     (lambda k, b=buf, j=jof:
                                      b[:, j + k * 128:j + (k + 1) * 128]),
                                     (lambda k, tt: xn_sl(k, 128 + tt * 512, 512)),
                                     [t_w, t_xn])
                        for tt in range(2):
                            bkA, bkG = bksA[tt], bksG[tt]
                            gts = gt_scr[:, tt * 512:(tt + 1) * 512]
                            ats = at_scr[:, tt * 512:(tt + 1) * 512]
                            bc = ex * 48
                            tg = em.op("act", lambda e, o=gts, i=bkG.ap[:],
                                       b=be1_sb[:, bc + 24 + f:bc + 25 + f]:
                                       e.activation(o, i, AF.Gelu, bias=b, scale=1.0),
                                       [bkG.ready, gwar[tt]],
                                       inc="act")
                            bkG.free = tg
                            ta2 = em.op("act", lambda e, o=ats, i=bkA.ap[:],
                                        b=be1_sb[:, bc + f:bc + 1 + f]:
                                        e.activation(o, i, AF.Identity, bias=b, scale=1.0),
                                        [bkA.ready, gwar[tt]], inc="act")
                            bkA.free = ta2
                            tm = em.op("dve", lambda e,
                                       o=ar(A_ACT + f * 1024 + tt * 512, 512),
                                       a=ats, g=gts:
                                       e.tensor_tensor(o, a, g, ALU.mult),
                                       [tg, ta2], inc="dve")
                            gwar[tt] = tm
                        e1free[f % 2] = gwar[1]
                    t_act_all = ("dve", em.cnt["dve"])
                    t_we2_tok = ("e2a", em.cnt["e2a"])
                    for p2 in range(2):
                        for f in range(FE):
                            for sub in range(4):
                                for n in range(2):
                                    bk = P[sub * 2 + n]
                                    l = ar(A_ACT + f * 1024 + p2 * 512 + sub * 128, 128)
                                    r = ar(A_E2 + f * 1024 + n * 512, 512)
                                    w = ([t_we2_tok, t_act_all, bk.free]
                                         if f == 0 else ())
                                    em.op("pe", lambda e, t=bk.ap[:], l=l, r=r,
                                          st=(f == 0):
                                          e.matmul(t, l, r, start=st, stop=False),
                                          w)
                        for sub in range(4):
                            for n in range(2):
                                bk = P[sub * 2 + n]
                                tok = em.op("pe", lambda e, t=bk.ap[:],
                                            r=be2row[0:1, n * 512:(n + 1) * 512]:
                                            e.matmul(t, ones_bf[0:1, :], r,
                                                     start=False, stop=True),
                                            [t_be2], inc="pe")
                                bk.ready = tok
                                tc = p2 * 4 + sub
                                av = acc[:, tc * 1024 + n * 512:
                                         tc * 1024 + (n + 1) * 512]
                                wc = wtok[:, tc * NE0 + ex:tc * NE0 + ex + 1]
                                tev = em.op("dve", lambda e, o=av, i=bk.ap[:], s=wc:
                                            e.scalar_tensor_tensor(o, i, s, o,
                                                                   ALU.mult, ALU.add),
                                            [tok, t_wtok], inc="dve")
                                bk.free = tev
                        if p2 == 1:
                            e2free = ("pe", em.cnt["pe"])

            if "L1" in layers:
                l1_barrier = em.snap()
                chunks = ln_chunks(q, False)
                if "L0" in layers:
                    tok_srcs = [("dve", em.cnt["dve"])] * 8
                else:
                    tok_srcs = [t_main] * 8
                emit_ln(q, chunks, tok_srcs, update_acc=True)
                t_xn1 = ("dve", em.cnt["dve"])

                t_cb = em.dma("gp", ar(A_CB, 8192).rearrange("p (i j) -> p i j", j=1024),
                              cbind.rearrange("(i p) j -> p i j", p=128), l1_barrier, sem="cb")
                t_cc = em.dma("gp", ar(A_CC, 8192).rearrange("p (i j) -> p i j", j=1024),
                              ccorr.rearrange("(i p) j -> p i j", p=128), l1_barrier, sem="cc")
                for j in range(KC):
                    par = j % 2
                    bks = (P[par * 2], P[par * 2 + 1])
                    bscr = f32p[:, 2 * 1024:3 * 1024]
                    for tt in range(2):
                        bk = bks[tt]
                        items = []
                        for iC in range(KC):
                            items.append((bk.ap[:],
                                          ar(A_CB + iC * 1024 + j * 128, 128),
                                          xn_sl(iC, 128 + tt * 512, 512)))
                        tok = pe_group(em, bk, items, [t_cb, t_xn1])
                        tev = em.op("dve", lambda e, o=bscr[:, tt * 512:(tt + 1) * 512],
                                    i=bk.ap[:]: e.tensor_copy(o, i), [tok],
                                    inc="dve")
                        bk.free = tev
                    dsp()
                    tsc = em.op("dve", lambda e, o=f32p[:, 3072:4096], d0=dtile[:], d1=bscr,
                                ini=carries[:, j:j + 1]:
                                e.tensor_tensor_scan(o, d0, d1, ini, ALU.mult,
                                                     ALU.add), [t_dtile], inc="dve")
                    dsp()
                    em.op("dve", lambda e, o=carries[:, j:j + 1],
                          i=f32p[:, 4095:4096]: e.tensor_copy(o, i))
                    em.op("dve", lambda e, o=ar(A_HOLO + j * 1024, 1024), i=f32p[:, 3072:4096]:
                          e.tensor_copy(o, i), inc="dve")
                t_holo = ("dve", em.cnt["dve"])
                for tc in range(8):
                    par = tc % 2
                    bks = (P[4 + par * 2], P[5 + par * 2])
                    for n in range(2):
                        bk = bks[n]
                        items = []
                        for iC in range(KC):
                            items.append((bk.ap[:],
                                          ar(A_HOLO + iC * 1024 + tc * 128, 128),
                                          ar(A_CC + iC * 1024 + n * 512, 512)))
                        tok = pe_group(em, bk, items, [t_cc, t_holo])
                        av = acc[:, tc * 1024 + n * 512:tc * 1024 + (n + 1) * 512]
                        tev = em.op("dve", lambda e, o=av, i=bk.ap[:], s=grep[:]:
                                    e.scalar_tensor_tensor(o, i, s, o, ALU.mult,
                                                           ALU.add),
                                    [tok, t_grep], inc="dve")
                        bk.free = tev

                prev_tex1 = None
                for tc in range(8):
                    bk = P[tc % 2]
                    items = []
                    for k in range(KC):
                        items.append((bk.ap[:, 0:16],
                                      xn_sl(k, 128 + tc * 128, 128),
                                      rwt_sb[:, k * 16:(k + 1) * 16]))
                    tok = pe_group(em, bk, items, [t_rwt, t_xn1])
                    trr = em.op("dve", lambda e, o=r_scr[:], i=bk.ap[:, 0:16]:
                                e.tensor_copy(o, i), [tok, prev_tex1], inc="dve")
                    bk.free = trr
                    dsp()
                    tmx = em.op("dve", lambda e, o=wstat[:, 0:1], i=r_scr[:]:
                                e.reduce_max(o, i, AX.X, negate=True), inc="dve")
                    tex = em.op("act", lambda e, o=r_scr[:], i=r_scr[:],
                                b=wstat[:, 0:1]:
                                e.activation(o, i, AF.Exp, bias=b, scale=1.0),
                                [tmx], inc="act")
                    prev_tex1 = tex
                    em.op("dve", lambda e, o=wstat[:, 1:2], i=r_scr[:]:
                          e.reduce_sum(o, i, AX.X), [tex])
                    dsp()
                    em.op("dve", lambda e, o=wstat[:, 2:3], i=wstat[:, 1:2]:
                          e.reciprocal(o, i))
                    dsp()
                    em.op("dve", lambda e, o=r_scr[:], i=r_scr[:], s=wstat[:, 2:3]:
                          e.tensor_scalar(o, i, s, None, ALU.mult))
                    dsp()
                    em.op("dve", lambda e, o=m8[:], i=r_scr[:]: e.max(o, i))
                    dsp()
                    em.op("dve", lambda e, o=eqA[:], i=r_scr[:], t4=m8[:, 3:4]:
                          e.tensor_scalar(o, i, t4, None, ALU.is_ge))
                    dsp()
                    em.op("dve", lambda e, o=gates[:, tc * 16:(tc + 1) * 16],
                          a=r_scr[:], b=eqA[:]:
                          e.tensor_tensor(o, a, b, ALU.mult), inc="dve")
                t_gates = ("dve", em.cnt["dve"])

                supv = supt.rearrange("s (k p) e -> s p k e", p=128)
                sdnv = sdnt.rearrange("s (f p) n -> s p f n", p=128)
                rupv = rupt.rearrange("s (k p) e -> s p k e", p=128)
                rdnv = rdnt.rearrange("s (f p) n -> s p f n", p=128)
                upbufs = (ar(A_SUPA, 4096), ar(A_SUPB, 4096))
                dnbufs = (ar(A_SDNA, 4096), ar(A_SDNB, 4096))
                upfree = [None, None]
                dnfree = [None, None]
                ffn1_barrier = em.snap()
                for xi in range(18):
                    par = xi % 2
                    if xi < 2:
                        upsrc, dnsrc, gcol = supv[xi], sdnv[xi], None
                    else:
                        r = xi - 2
                        upsrc, dnsrc, gcol = rupv[r], rdnv[r], r
                    t_up = em.dma("gp",
                                  upbufs[par].rearrange("p (k e) -> p k e", e=512),
                                  upsrc, [upfree[par]] + (ffn1_barrier if xi < 2 else []),
                                  sem=("ua", "ub")[par])
                    t_dn = em.dma("gp",
                                  dnbufs[par].rearrange("p (f n) -> p f n", n=1024),
                                  dnsrc, [dnfree[par]] + (ffn1_barrier if xi < 2 else []),
                                  sem=("da", "db")[par])
                    tok_g = {}
                    for ec in range(4):
                        for tt in range(2):
                            bk = P[(ec % 2) * 2 + tt]
                            items = []
                            for k in range(KC):
                                items.append((bk.ap[:],
                                              upbufs[par][:, k * 512 + ec * 128:
                                                          k * 512 + (ec + 1) * 128],
                                              xn_sl(k, 128 + tt * 512, 512)))
                            tok = pe_group(em, bk, items, [t_up, t_xn1])
                            tg = em.op("act", lambda e,
                                       o=ar(A_ACTR + ec * 1024 + tt * 512, 512),
                                       i=bk.ap[:]:
                                       e.activation(o, i, AF.Gelu, bias=0.0,
                                                    scale=1.0), [tok], inc="act")
                            bk.free = tg
                            tok_g[(ec, tt)] = tg
                    upfree[par] = ("act", em.cnt["act"])
                    t_actr = ("act", em.cnt["act"])
                    for p2 in range(2):
                        for sub in range(4):
                            for n in range(2):
                                bk = P[sub * 2 + n]
                                items = []
                                for ec in range(4):
                                    items.append((bk.ap[:],
                                                  ar(A_ACTR + ec * 1024 + p2 * 512
                                                     + sub * 128, 128),
                                                  dnbufs[par][:, ec * 1024 + n * 512:
                                                              ec * 1024 + (n + 1) * 512]))
                                tok = pe_group(em, bk, items,
                                               [t_dn, t_actr])
                                tc = p2 * 4 + sub
                                av = acc[:, tc * 1024 + n * 512:
                                         tc * 1024 + (n + 1) * 512]
                                if gcol is None:
                                    tev = em.op("dve", lambda e, o=av, i=bk.ap[:]:
                                                e.tensor_tensor(o, o, i, ALU.add),
                                                [tok], inc="dve")
                                else:
                                    gc = gates[:, tc * 16 + gcol:tc * 16 + gcol + 1]
                                    tev = em.op("dve", lambda e, o=av, i=bk.ap[:],
                                                s=gc:
                                                e.scalar_tensor_tensor(
                                                    o, i, s, o, ALU.mult, ALU.add),
                                                [tok, t_gates], inc="dve")
                                bk.free = tev
                    dnfree[par] = ("dve", em.cnt["dve"])

            store_barrier = em.snap()
            em.dma("sp", dst[qs:qs + QT, :].rearrange("(c p) n -> p c n", p=128),
                   acc[:].rearrange("p (c n) -> p c n", n=1024), store_barrier, sem="st")

        if dump:
            dump_o = nc.dram_tensor("dump_arena", [128, 57344], BF16,
                                    kind="ExternalOutput").ap()
            dump_x = nc.dram_tensor("dump_xn", [128, KC * XNW], BF16,
                                    kind="ExternalOutput").ap()
            em.limit = None
            snapf = em.snap()
            em.dma("sp", dump_o, arena[:], snapf, sem="st")
            em.dma("sp", dump_x, xn[:], snapf, sem="st")
        for s in ("dio", "sx", "st"):
            em.final_wait("sp", s)
        for s in ("dwm", "wW", "s1a", "s1b", "w2s", "e1a", "e1b", "e2a",
                  "e2b", "be2", "cb", "cc", "ua", "ub", "da", "db"):
            em.final_wait("gp", s)

        @block.sync
        def _(eng):
            for f in em.ops["sp"]:
                f(eng, sems)

        @block.gpsimd
        def _(eng):
            for f in em.ops["gp"]:
                f(eng, sems)

        @block.tensor
        def _(eng):
            for f in em.ops["pe"]:
                f(eng, sems)

        @block.scalar
        def _(eng):
            for f in em.ops["act"]:
                f(eng, sems)

        @block.vector
        def _(eng):
            for f in em.ops["dve"]:
                f(eng, sems)

    return nc


def _host_prep(inputs):
    g = {k: np.asarray(v, dtype=np.float32) for k, v in inputs.items()}
    idxf = np.arange(D)
    roll_conv = (idxf[None, :] - idxf[:, None]) % D
    roll_corr = (idxf[:, None] - idxf[None, :]) % D
    c = lambda a: np.ascontiguousarray(a, dtype=np.float32)
    common = dict(
        wq=c(g["L0_attn_Wq"]), wk=c(g["L0_attn_Wk"]),
        wv=c(g["L0_attn_Wv"]), wo=c(g["L0_attn_Wo"]),
        ws1=c(g["L0_ffn_Ws1"]), ws2=c(g["L0_ffn_Ws2"]),
        we1=c(g["L0_ffn_We1"]), we2=c(g["L0_ffn_We2"]),
        wr=c(g["L0_ffn_Wr"]),
        bq_c=c(g["L0_attn_bq"].reshape(8, 128).T),
        bk_c=c(g["L0_attn_bk"].reshape(8, 128).T),
        bs1_c=c(g["L0_ffn_bs1"].reshape(32, 128).T),
        be1_c=c(g["L0_ffn_be1"].reshape(NE0, 48, 128).transpose(2, 0, 1)
                .reshape(128, NE0 * 48)),
        bv_r=c(g["L0_attn_bv"].reshape(1, D)),
        bo_r=c(g["L0_attn_bo"].reshape(1, D)),
        bs2_r=c(g["L0_ffn_bs2"].reshape(1, D)),
        br_r=c(g["L0_ffn_br"].reshape(1, NE0)),
        be2=c(g["L0_ffn_be2"]),
        cbind=c(g["L1_attn_bind_key"][roll_conv]),
        ccorr=c(g["L1_attn_unbind_key"][roll_corr]),
        supt=c(g["L1_ffn_shared_up"].transpose(0, 2, 1)),
        sdnt=c(g["L1_ffn_shared_down"].transpose(0, 2, 1)),
        rupt=c(g["L1_ffn_routed_up"].transpose(0, 2, 1)),
        rdnt=c(g["L1_ffn_routed_down"].transpose(0, 2, 1)),
        rwt=c(g["L1_ffn_router_w"].T),
        decay_rep=c(np.broadcast_to(g["L1_attn_decay"].reshape(1, 1), (128, 1))),
        gate_rep=c(np.broadcast_to(g["L1_attn_gate"].reshape(1, 1), (128, 1))),
    )
    x = c(g["L0_attn_x"])
    return common, x


def kernel(**inputs):
    common, x = _host_prep(inputs)
    ncores = 8
    if FUSED:
        nc = build_nc(("L0", "L1"))
        in_maps = [dict(common, src=x[b]) for b in range(ncores)]
        res = run_bass_kernel_spmd(nc, in_maps, core_ids=list(range(ncores)))
        out = np.stack([np.asarray(res.results[b]["dst"]) for b in range(ncores)])
    else:
        nc0 = build_nc(("L0",))
        in_maps = [dict(common, src=x[b]) for b in range(ncores)]
        r0 = run_bass_kernel_spmd(nc0, in_maps, core_ids=list(range(ncores)))
        mid = [np.asarray(r0.results[b]["dst"]) for b in range(ncores)]
        nc1 = build_nc(("L1",))
        in_maps = [dict(common, src=mid[b]) for b in range(ncores)]
        r1 = run_bass_kernel_spmd(nc1, in_maps, core_ids=list(range(ncores)))
        out = np.stack([np.asarray(r1.results[b]["dst"]) for b in range(ncores)])
    return out.astype(np.float32)


# version 04 (final)
# speedup vs baseline: 1.2426x; 1.0973x over previous
import contextlib
import numpy as np
import ml_dtypes
import concourse.bass as bass
import concourse.mybir as mybir
from concourse.bass_utils import run_bass_kernel_spmd

F32 = mybir.dt.float32
BF16 = mybir.dt.bfloat16
AF = mybir.ActivationFunctionType
ALU = mybir.AluOpType
AX = mybir.AxisListType

D = 1024
S = 4096
QT = 1024
NQ = S // QT
KC = D // 128
XNW = QT + 256
NE0 = 12
FE = 24
EPS = 1e-5

FUSED = True


class Em:
    ENGS = ("sp", "gp", "pe", "act", "dve")
    DMA_SEMS = ("dio", "dwm", "sx", "st", "wW", "s1a", "s1b", "w2s",
                "e1a", "e1b", "e2a", "e2b", "be2", "cb", "cc",
                "ua", "ub", "da", "db", "v2")
    SEMS = ("pe", "act", "dve") + DMA_SEMS

    def __init__(self, limit=None):
        self.ops = {e: [] for e in self.ENGS}
        self.cnt = dict.fromkeys(self.SEMS, 0)
        self.wm = {}
        self.seq = 0
        self.limit = limit

    def _filter(self, eng, waits):
        out = []
        stack = list(waits)
        while stack:
            t = stack.pop()
            if t is None:
                continue
            if isinstance(t, list):
                stack.extend(t)
                continue
            s, v = t
            if v <= self.wm.get((eng, s), 0):
                continue
            self.wm[(eng, s)] = v
            out.append((s, v))
        return out

    def op(self, eng, fn, waits=(), inc=None, n=1):
        self.seq += 1
        if self.limit is not None and self.seq > self.limit:
            return (inc, self.cnt[inc]) if inc else None
        ws = self._filter(eng, waits)
        tok = None
        if inc:
            self.cnt[inc] += n
            tok = (inc, self.cnt[inc])

        def replay(eo, sems, _ws=ws, _fn=fn, _inc=inc, _n=n):
            for s, v in _ws:
                eo.wait_ge(sems[s], v)
            ins = _fn(eo)
            if _inc:
                ins.then_inc(sems[_inc], _n)

        self.ops[eng].append(replay)
        return tok

    def dma(self, eng, out_ap, in_ap, waits=(), sem=None):
        if sem is None:
            sem = "dwm" if eng == "gp" else "dio"
        return self.op(eng, lambda e, o=out_ap, i=in_ap: e.dma_start(out=o, in_=i),
                       waits, inc=sem, n=16)

    def snap(self):
        return [(s, v) for s, v in self.cnt.items() if v > 0]

    def barrier_all(self, engs=None, always=False):
        snap = self.snap()
        for eng in (engs or self.ENGS):
            ws = self._filter(eng, snap)
            if ws:
                def rp(eo, sems, _ws=ws):
                    for s, v in _ws:
                        eo.wait_ge(sems[s], v)
                self.ops[eng].append(rp)

    def final_wait(self, eng, sem):
        v = self.cnt[sem]
        fw = lambda eo, sems, s=sem, vv=v: eo.wait_ge(sems[s], vv)
        self.ops[eng].append(fw)


class Bank:
    def __init__(self, ap):
        self.ap = ap
        self.free = None
        self.ready = None


def pe_group(em, bank, items, waits=(), ready=True):
    n = len(items)
    tok = None
    for i, (tgt, l, r) in enumerate(items):
        w = list(waits) + [bank.free] if i == 0 else ()
        tok = em.op(
            "pe",
            (lambda e, t=tgt, l=l, r=r, st=(i == 0), sp=(i == n - 1):
             e.matmul(t, l, r, start=st, stop=sp)),
            w, inc="pe" if i == n - 1 else None)
    if ready:
        bank.ready = tok
    return tok


def pe_inter(em, banks, lhs_fn, rhs_fn, waits=(), nk=8):
    for k in range(nk):
        for i, bk in enumerate(banks):
            w = (list(waits) + [bk.free]) if k == 0 else ()
            tok = em.op(
                "pe",
                (lambda e, t=bk.ap[:], l=lhs_fn(k), r=rhs_fn(k, i),
                 st=(k == 0), sp=(k == nk - 1):
                 e.matmul(t, l, r, start=st, stop=sp)),
                w, inc="pe" if k == nk - 1 else None)
            if k == nk - 1:
                bk.ready = tok


def ttiles(t0, t1, w=512):
    out = []
    t = t0
    while t < t1:
        out.append((t, min(w, t1 - t)))
        t += w
    return out


def build_nc(layers=("L0", "L1"), nquarters=NQ, detect_races=True, ph=99, trunc=None,
             dump=False):
    nc = bass.Bass("TRN2", target_bir_lowering=False,
                   detect_race_conditions=detect_races)
    dt_in = {}

    def din(name, shape, dt=F32):
        dt_in[name] = nc.dram_tensor(name, list(shape), dt, kind="ExternalInput").ap()
        return dt_in[name]

    src = din("src", (S, D))
    dst = nc.dram_tensor("dst", [S, D], F32, kind="ExternalOutput").ap()

    wq = din("wq", (D, D)); wk = din("wk", (D, D)); wv = din("wv", (D, D)); wo = din("wo", (D, D))
    ws1 = din("ws1", (D, 4096)); ws2 = din("ws2", (2048, D))
    we1 = din("we1", (NE0, D, 6144)); we2 = din("we2", (NE0, 3072, D))
    wr = din("wr", (D, NE0))
    bq_c = din("bq_c", (128, 8)); bk_c = din("bk_c", (128, 8))
    bs1_c = din("bs1_c", (128, 32)); be1_c = din("be1_c", (128, 576))
    bv_r = din("bv_r", (1, D)); bo_r = din("bo_r", (1, D)); bs2_r = din("bs2_r", (1, D))
    br_r = din("br_r", (1, NE0)); be2_d = din("be2", (NE0, D))
    cbind = din("cbind", (D, D)); ccorr = din("ccorr", (D, D))
    supt = din("supt", (2, D, 512)); sdnt = din("sdnt", (2, 512, D))
    rupt = din("rupt", (16, D, 512)); rdnt = din("rdnt", (16, 512, D))
    rwt = din("rwt", (D, 16))
    decay_rep = din("decay_rep", (128, 1)); gate_rep = din("gate_rep", (128, 1))

    ident_d = nc.inline_tensor(np.eye(128, dtype=np.float32), name="ident")
    counts = np.ones(S, np.float32) * 2.0
    counts[:64] = 1.0
    counts[-64:] = 1.0
    rcw_np = np.zeros((128, 63), np.float32)
    for w in range(63):
        rcw_np[:, w] = 1.0 / counts[64 * w:64 * w + 128]
    rcw_d = nc.inline_tensor(rcw_np, name="rcw")
    r64_np = np.roll(np.eye(128, dtype=np.float32), 64, axis=1)
    r64f_d = nc.inline_tensor(r64_np, name="r64f")

    em = Em(limit=trunc)

    with contextlib.ExitStack() as ctx:
        E = ctx.enter_context

        acc = E(nc.sbuf_tensor("acc", [128, 8 * 1024], F32))
        xn = E(nc.sbuf_tensor("xn", [128, KC * XNW], BF16))
        arena = E(nc.sbuf_tensor("arena", [128, 57344], BF16))
        f32p = E(nc.sbuf_tensor("f32p", [128, 4 * 1024], F32))
        dtile = E(nc.sbuf_tensor("dtile", [128, 1024], F32))
        stats = E(nc.sbuf_tensor("stats", [128, 64], F32))
        wstat = E(nc.sbuf_tensor("wstat", [128, 16], F32))
        wtok = E(nc.sbuf_tensor("wtok", [128, 96], F32))
        gates = E(nc.sbuf_tensor("gates", [128, 128], F32))
        r_scr = E(nc.sbuf_tensor("r_scr", [128, 16], F32))
        m8 = E(nc.sbuf_tensor("m8", [128, 8], F32))
        eqA = E(nc.sbuf_tensor("eqA", [128, 16], F32))
        eqB = E(nc.sbuf_tensor("eqB", [128, 16], F32))
        carries = E(nc.sbuf_tensor("carries", [128, 8], F32))
        dcol = E(nc.sbuf_tensor("dcol", [128, 2], F32))
        epsc = E(nc.sbuf_tensor("epsc", [128, 1], F32))
        p32 = E(nc.sbuf_tensor("p32", [128, 2 * 128], F32))
        pT_sb = E(nc.sbuf_tensor("pT_sb", [128, 2 * 128], BF16))
        gt_scr = E(nc.sbuf_tensor("gt_scr", [128, 2 * 512], BF16))
        at_scr = E(nc.sbuf_tensor("at_scr", [128, 2 * 512], BF16))
        bqk_sb = E(nc.sbuf_tensor("bqk_sb", [128, 16], F32))
        bs1_sb = E(nc.sbuf_tensor("bs1_sb", [128, 32], F32))
        be1_sb = E(nc.sbuf_tensor("be1_sb", [128, 576], F32))
        rcw_sb = E(nc.sbuf_tensor("rcw_sb", [128, 63], F32))
        ident_sb = E(nc.sbuf_tensor("ident_sb", [128, 128], F32))
        r64f_sb = E(nc.sbuf_tensor("r64f_sb", [128, 128], F32))
        cp32 = E(nc.sbuf_tensor("cp32", [128, 2 * 128], F32))
        wr_sb = E(nc.sbuf_tensor("wr_sb", [128, 96], BF16))
        rwt_sb = E(nc.sbuf_tensor("rwt_sb", [128, 128], BF16))
        drep = E(nc.sbuf_tensor("drep", [128, 1], F32))
        grep = E(nc.sbuf_tensor("grep", [128, 1], F32))
        ones_bf = E(nc.sbuf_tensor("ones_bf", [1, 128], BF16))
        bv_sb = E(nc.sbuf_tensor("bv_sb", [1, D], BF16))
        bo_sb = E(nc.sbuf_tensor("bo_sb", [1, D], BF16))
        bs2_sb = E(nc.sbuf_tensor("bs2_sb", [1, D], BF16))
        br_sb = E(nc.sbuf_tensor("br_sb", [1, NE0], BF16))
        be2_sb = E(nc.sbuf_tensor("be2_sb", [NE0, D], BF16))
        be2row = E(nc.sbuf_tensor("be2row", [1, D], BF16))

        P = [Bank(E(nc.psum_tensor(f"P{i}", [128, 512], F32))) for i in range(8)]
        spacer = E(nc.sbuf_tensor("spacer", [128, 2], F32))

        def dsp():
            em.op("dve", lambda e: e.tensor_scalar(spacer[:, 0:1], spacer[:, 0:1],
                                                   1.0, None, ALU.mult))

        def asp():
            em.op("act", lambda e: e.activation(spacer[:, 1:2], spacer[:, 1:2],
                                                AF.Identity, bias=0.0, scale=1.0))

        sems = {}
        for sname in Em.SEMS:
            sems[sname] = E(nc.semaphore(sname))
        block = E(nc.Block())

        def ar(c0, w):
            return arena[:, c0:c0 + w]
        A_QT, A_KT, A_VT, A_V2 = 0, 10240, 20480, 30720
        A_OT, A_WA = 39936, 48128
        A_ACT = 0
        A_WS2 = 16384
        A_E2 = 24576
        A_E1A, A_E1B = 49152, 53248
        A_CB, A_CC, A_HOLO = 0, 8192, 16384
        A_ACTR = 0
        A_SUPA, A_SUPB = 4096, 8192
        A_SDNA, A_SDNB = 12288, 16384 + 4096

        def xn_sl(k, t0, w):
            return xn[:, k * XNW + t0:k * XNW + t0 + w]

        def ln_chunks(q, with_halo):
            lo = 1 if (q > 0 and with_halo) else 0
            hi = 1 if (q < 3 and with_halo) else 0
            ch = []
            if lo:
                ch.append((f32p[:, 0:1024], 0))
            for c in range(8):
                ch.append((acc[:, c * 1024:(c + 1) * 1024], 128 + c * 128))
            if hi:
                ch.append((f32p[:, 1024:2048], 1152))
            return ch

        def emit_ln(q, chunks, tok_srcs, update_acc):
            last = None
            tp_done = {}
            for idx, (src, t_loc) in enumerate(chunks):
                sc = stats[:, idx * 6:idx * 6 + 6]
                par = idx % 2
                tsrc = tok_srcs[idx] if idx < len(tok_srcs) else None
                xtsq = f32p[:, 2 * 1024:3 * 1024]
                em.op("dve", lambda e, o=xtsq, i=src:
                      e.tensor_tensor(o, i, i, ALU.mult),
                      [tsrc, tp_done.get(0)])
                em.op("dve", lambda e, o=sc[:, 0:1], i=src: e.reduce_sum(o, i, AX.X))
                em.op("dve", lambda e, o=sc[:, 2:3], i=xtsq:
                      e.reduce_sum(o, i, AX.X))
                em.op("dve", lambda e, o=sc[:, 1:2], i=sc[:, 0:1]:
                      e.tensor_scalar(o, i, -1.0 / D, None, ALU.mult))
                em.op("dve", lambda e, o=sc[:, 4:5], i=sc[:, 2:3]:
                      e.tensor_scalar(o, i, 1.0 / D, None, ALU.mult))
                em.op("dve", lambda e, o=sc[:, 3:4], i=sc[:, 1:2]:
                      e.tensor_scalar(o, i, i, None, ALU.mult))
                dsp()
                tvar = em.op("dve", lambda e, o=sc[:, 3:4], i=sc[:, 4:5], m=sc[:, 3:4]:
                             e.tensor_scalar(o, i, m, None, ALU.subtract),
                             (), inc="dve")
                em.op("act", lambda e, o=sc[:, 3:4], i=sc[:, 3:4]:
                      e.activation(o, i, AF.Ln, bias=epsc[:], scale=1.0), [tvar])
                asp()
                trstd = em.op("act", lambda e, o=sc[:, 3:4], i=sc[:, 3:4]:
                              e.activation(o, i, AF.Exp, bias=0.0, scale=-0.5),
                              inc="act")
                xt = f32p[:, 2 * 1024:3 * 1024]
                txt = em.op("dve", lambda e, o=xt, i=src, m=sc[:, 1:2], r=sc[:, 3:4]:
                            e.tensor_scalar(o, i, m, r, ALU.add, ALU.mult),
                            [trstd, tp_done.get(0)], inc="dve")
                ttp = None
                for j in range(KC):
                    bk = P[4 + (j % 4)]
                    sub = bk.ap[:, 0:128]
                    ttp = em.op("pe", lambda e, o=sub, i=xt[:, j * 128:(j + 1) * 128]:
                                e.transpose(o, i, ident_sb[:]), [txt, bk.free],
                                inc="pe")
                    bk.ready = ttp
                    tev = em.op("dve", lambda e, o=xn_sl(j, t_loc, 128), i=sub:
                                e.tensor_copy(o, i), [ttp], inc="dve")
                    bk.free = tev
                    last = tev
                tp_done[0] = ttp
                if update_acc:
                    em.op("dve", lambda e, o=sc[:, 0:1], i=sc[:, 3:4]:
                          e.tensor_scalar(o, i, 1.0, None, ALU.add))
                    em.op("dve", lambda e, o=sc[:, 1:2], i=sc[:, 1:2], r=sc[:, 3:4]:
                          e.tensor_scalar(o, i, r, None, ALU.mult))
                    dsp()
                    last = em.op("dve", lambda e, o=src, i=src, a=sc[:, 0:1],
                                 b=sc[:, 1:2]:
                                 e.tensor_scalar(o, i, a, b, ALU.mult, ALU.add),
                                 inc="dve")
            return last

        def down_proj(name, nf, act_off, wb_ap, tok_w, evac, extra_waits=(),
                      group_starts=None):
            for p2 in range(2):
                for f in range(nf):
                    for sub in range(4):
                        for n in range(2):
                            bk = P[sub * 2 + n]
                            tgt = bk.ap[:]
                            l = ar(act_off + f * 1024 + p2 * 512 + sub * 128, 128)
                            r = wb_ap(f, n)
                            first = (f == 0)
                            w = (list(extra_waits) + [tok_w(f), bk.free]) if first else \
                                ([tok_w(f)] if (group_starts and f in group_starts) else ())
                            em.op("pe", lambda e, t=tgt, l=l, r=r, st=first:
                                  e.matmul(t, l, r, start=st, stop=False), w)
                for sub in range(4):
                    for n in range(2):
                        bk = P[sub * 2 + n]
                        yield ("close", p2, sub, n, bk)

        em.op("dve", lambda e: e.memset(spacer[:], 0.0))
        em.op("dve", lambda e: e.memset(ones_bf[:], 1.0))
        em.op("dve", lambda e: e.memset(epsc[:], EPS))
        em.op("dve", lambda e: e.memset(xn[:], 0.0))
        em.op("dve", lambda e: e.memset(arena[:], 0.0))
        em.op("dve", lambda e: e.memset(carries[:], 0.0), inc="dve")
        t_ident = em.dma("sp", ident_sb[:], ident_d.ap())
        em.dma("sp", r64f_sb[:], r64f_d.ap())
        t_rcw = em.dma("sp", rcw_sb[:], rcw_d.ap())
        em.dma("sp", bqk_sb[:, 0:8], bq_c)
        em.dma("sp", bqk_sb[:, 8:16], bk_c)
        em.dma("sp", bs1_sb[:], bs1_c)
        em.dma("sp", be1_sb[:], be1_c)
        t_drep = em.dma("sp", drep[:], decay_rep)
        t_grep = em.dma("sp", grep[:], gate_rep)
        em.dma("gp", bv_sb[:], bv_r)
        em.dma("gp", bo_sb[:], bo_r)
        em.dma("gp", bs2_sb[:], bs2_r)
        em.dma("gp", br_sb[:], br_r)
        em.dma("gp", be2_sb[:], be2_d)
        em.dma("gp", wr_sb[:].rearrange("p (k e) -> p k e", e=NE0),
               wr.rearrange("(k p) e -> p k e", p=128))
        t_rwt = em.dma("gp", rwt_sb[:].rearrange("p (k e) -> p k e", e=16),
                       rwt.rearrange("(k p) e -> p k e", p=128))
        ta = em.op("act", lambda e: e.activation(dcol[:, 0:1], drep[:], AF.Exp,
                                                 bias=0.0, scale=-1.0),
                   [("dio", em.cnt["dio"])], inc="act")
        tb = em.op("dve", lambda e: e.tensor_scalar(dcol[:, 0:1], dcol[:, 0:1], 1.0,
                                                    None, ALU.add), [ta])
        dsp()
        tc_ = em.op("dve", lambda e: e.reciprocal(dcol[:, 1:2], dcol[:, 0:1]))
        em.op("dve", lambda e: e.memset(dtile[:], 1.0))
        dsp()
        t_dtile = em.op("dve", lambda e: e.tensor_scalar(dtile[:], dtile[:],
                                                         dcol[:, 1:2], None, ALU.mult),
                        inc="dve")

        for q in range(nquarters):
            qs = q * QT
            lo = 1 if q > 0 else 0
            hi = 1 if q < 3 else 0
            frame0 = qs - 128
            em.barrier_all()
            barrier = em.snap()

            em.dma("sp",
                   acc[:].rearrange("p (c n) -> p c n", n=1024),
                   src[qs:qs + QT, :].rearrange("(c p) n -> p c n", p=128),
                   barrier, sem="sx")
            if "L0" in layers:
                if lo:
                    em.dma("sp", f32p[:, 0:1024],
                           src[qs - 128:qs, :], barrier, sem="sx")
                if hi:
                    em.dma("sp", f32p[:, 1024:2048],
                           src[qs + QT:qs + QT + 128, :], barrier, sem="sx")
            t_main = t_hx0 = t_hx1 = ("sx", em.cnt["sx"])

            if "L0" in layers:
                chunks = ln_chunks(q, True)
                toks = []
                for (srcap, t_loc) in chunks:
                    if t_loc == 0:
                        toks.append(t_hx0)
                    elif t_loc == 1152:
                        toks.append(t_hx1)
                    else:
                        toks.append(t_main)
                t_xn = emit_ln(q, chunks, toks, update_acc=False)

                tmin = 0 if lo else 128
                tmax = 1280 if hi else 1152
                t_wtok = ("dve", em.cnt["dve"])

            if "L0" in layers and ph >= 1:
                t_wq = em.dma("gp", ar(A_WA, 8192).rearrange("p (k m) -> p k m", m=1024),
                              wq.rearrange("(k p) m -> p k m", p=128), [t_ident], sem="wW")
                tiles = ttiles(tmin, tmax)

                def proj_fm(dst_off, t_w, bias_base, waits_extra):
                    tok_mm_last = None
                    for m in range(KC):
                        bks = [P[(m % 2) * 3 + i] for i in range(len(tiles))]
                        for ti, (t0, wdt) in enumerate(tiles):
                            bk = bks[ti]
                            items = []
                            for k in range(KC):
                                items.append((bk.ap[:, 0:wdt],
                                              ar(A_WA + k * 1024 + m * 128, 128),
                                              xn_sl(k, t0, wdt)))
                            tok = pe_group(em, bk, items,
                                           [t_w, t_xn] + list(waits_extra))
                            tok_mm_last = tok
                            tev = em.op("act",
                                        lambda e, o=ar(dst_off + m * XNW + t0, wdt),
                                        i=bk.ap[:, 0:wdt],
                                        b=bqk_sb[:, bias_base + m:bias_base + m + 1]:
                                        e.activation(o, i, AF.Identity, bias=b, scale=1.0),
                                        [tok], inc="act")
                            bk.free = tev
                    return tok_mm_last

                tql = proj_fm(A_QT, t_wq, 0, [])
                t_wk = em.dma("gp", ar(A_WA, 8192).rearrange("p (k m) -> p k m", m=1024),
                              wk.rearrange("(k p) m -> p k m", p=128), [("pe", tql[1])], sem="wW")
                tkl = proj_fm(A_KT, t_wk, 8, [])
                tqk = (("act", em.cnt["act"]))

                t_wv = em.dma("gp", ar(A_WA, 8192).rearrange("p (k m) -> p k m", m=1024),
                              wv.rearrange("(k p) m -> p k m", p=128), [("pe", tkl[1])], sem="wW")
                vc0, vc1 = tmin // 128, tmax // 128
                for vc in range(vc0, vc1):
                    par = vc % 2
                    bks = (P[par * 2], P[par * 2 + 1])
                    seq = {0: [], 1: []}
                    for k in range(KC):
                        for n in range(2):
                            seq[n].append((bks[n].ap[:],
                                           xn_sl(k, vc * 128, 128),
                                           ar(A_WA + k * 1024 + n * 512, 512)))
                    for n in range(2):
                        bk = bks[n]
                        items = seq[n]
                        ntot = len(items) + 1
                        for i, (tgt, l, r) in enumerate(items):
                            w = [t_wv, t_xn, bk.free] if i == 0 else ()
                            em.op("pe", lambda e, t=tgt, l=l, r=r, st=(i == 0):
                                  e.matmul(t, l, r, start=st, stop=False), w)
                        tok = em.op("pe", lambda e, t=bk.ap[:],
                                    r=bv_sb[0:1, n * 512:(n + 1) * 512]:
                                    e.matmul(t, ones_bf[0:1, :], r, start=False,
                                             stop=True), (), inc="pe")
                        bk.ready = tok
                        tev = em.op("dve", lambda e,
                                    o=ar(A_VT + vc * 1024 + n * 512, 512),
                                    i=bk.ap[:]: e.tensor_copy(o, i), [tok], inc="dve")
                        bk.free = tev
                t_v_all = ("dve", em.cnt["dve"])
                em.dma("gp", ar(A_V2, 9216)[0:64, :], ar(A_VT, 9216)[64:128, :],
                       [t_v_all], sem="v2")
                em.dma("gp", ar(A_V2, 9216)[64:128, :],
                       ar(A_VT + 1024, 9216)[0:64, :], [t_v_all], sem="v2")
                t_v2 = ("v2", em.cnt["v2"])

                t_ot0 = em.op("dve", lambda e: e.memset(ar(A_OT, 8192), 0.0),
                              [tqk], inc="dve")
                s_first = max(0, qs - 64)
                s_last = min(S - 128, qs + QT - 64)
                t_scores_last = None
                pf_war = [None, None]
                ws_war = [None, None]
                for wi, s in enumerate(range(s_first, s_last + 1, 64)):
                    par = wi % 2
                    t_l = s - frame0
                    w_glob = s // 64
                    off = t_l % 128
                    vc = t_l // 128
                    bkS, bkT = P[4 + par], P[6 + par]
                    items = []
                    for k in range(KC):
                        items.append((bkS.ap[:, 0:128],
                                      ar(A_QT + k * XNW + t_l, 128),
                                      ar(A_KT + k * XNW + t_l, 128)))
                    tok_s = pe_group(em, bkS, items, [tqk])
                    t_scores_last = tok_s
                    ws = wstat[:, par * 8:par * 8 + 8]
                    tmx = em.op("dve", lambda e, o=ws[:, 0:1], i=bkS.ap[:, 0:128]:
                                e.reduce_max(o, i, AX.X, negate=True),
                                [tok_s, ws_war[par]], inc="dve")
                    dsp()
                    em.op("dve", lambda e, o=ws[:, 1:2], i=ws[:, 0:1]:
                          e.tensor_scalar(o, i, 0.125, None, ALU.mult), inc="dve")
                    tdv = ("dve", em.cnt["dve"])
                    pf = p32[:, par * 128:(par + 1) * 128]
                    texp = em.op("act", lambda e, o=pf, i=bkS.ap[:, 0:128],
                                 b=ws[:, 1:2]:
                                 e.activation(o, i, AF.Exp, bias=b, scale=0.125),
                                 [tdv, t_rcw, pf_war[par]], inc="act")
                    bkS.free = texp
                    ws_war[par] = texp
                    em.op("dve", lambda e, o=ws[:, 2:3], i=pf:
                          e.reduce_sum(o, i, AX.X), [texp])
                    dsp()
                    em.op("dve", lambda e, o=ws[:, 3:4], i=ws[:, 2:3]:
                          e.reciprocal(o, i))
                    dsp()
                    em.op("dve", lambda e, o=ws[:, 3:4], i=ws[:, 3:4],
                          r=rcw_sb[:, w_glob:w_glob + 1]:
                          e.tensor_scalar(o, i, r, None, ALU.mult))
                    dsp()
                    tpn = em.op("dve", lambda e, o=pf, i=pf, r=ws[:, 3:4]:
                                e.tensor_scalar(o, i, r, None, ALU.mult), inc="dve")
                    idmat = ident_sb if off == 0 else r64f_sb
                    ttp = em.op("pe", lambda e, o=bkT.ap[:, 0:128], i=pf, m=idmat:
                                e.transpose(o, i, m[:]), [tpn, bkT.free],
                                inc="pe")
                    bkT.ready = ttp
                    pf_war[par] = ttp
                    pts = pT_sb[:, par * 128:(par + 1) * 128]
                    tpt = em.op("dve", lambda e, o=pts, i=bkT.ap[:, 0:128]:
                                e.tensor_copy(o, i), [ttp], inc="dve")
                    bkT.free = tpt
                    psO = (P[0 + 2 * par], P[1 + 2 * par])
                    for n in range(2):
                        bk = psO[n]
                        if off == 0:
                            vsl = ar(A_VT + vc * 1024 + n * 512, 512)
                        else:
                            vsl = ar(A_V2 + vc * 1024 + n * 512, 512)
                        items = [(bk.ap[:], pts, vsl)]
                        tok_o = pe_group(em, bk, items, [tpt, t_v_all, t_v2])
                        if off == 0:
                            c = (s - qs) // 128
                            ov = ar(A_OT + c * 1024 + n * 512, 512)
                            pv = bk.ap[:]
                            tsc = em.op("dve", lambda e, o=ov, i=pv:
                                        e.tensor_tensor(o, o, i, ALU.add),
                                        [tok_o, t_ot0], inc="dve")
                        else:
                            cA = (s - qs - 64) // 128
                            tsc = None
                            if cA >= 0:
                                ov = ar(A_OT + cA * 1024 + n * 512, 512)[64:128, :]
                                pv = bk.ap[:][64:128, :]
                                tsc = em.op("dve", lambda e, o=ov, i=pv:
                                            e.tensor_tensor(o, o, i, ALU.add),
                                            [tok_o, t_ot0],
                                            inc="dve")
                            if cA + 1 <= 7:
                                ov = ar(A_OT + (cA + 1) * 1024 + n * 512, 512)[0:64, :]
                                pv = bk.ap[:][0:64, :]
                                tsc = em.op("dve", lambda e, o=ov, i=pv:
                                            e.tensor_tensor(o, o, i, ALU.add),
                                            [tok_o, t_ot0],
                                            inc="dve")
                        bk.free = tsc
                t_scatter_all = ("dve", em.cnt["dve"])

                t_wo = em.dma("gp", ar(A_WA, 8192).rearrange("p (k m) -> p k m", m=1024),
                              wo.rearrange("(k p) m -> p k m", p=128),
                              [t_scores_last], sem="wW")
                cp_war = [None, None]
                for tc in range(8):
                    for k in range(KC):
                        bkt = P[4 + (k % 2)]
                        cps = cp32[:, (k % 2) * 128:(k % 2 + 1) * 128]
                        tcp = em.op("dve", lambda e, o=cps,
                                    i=ar(A_OT + tc * 1024 + k * 128, 128):
                                    e.tensor_copy(o, i),
                                    [t_scatter_all, cp_war[k % 2]], inc="dve")
                        ttp = em.op("pe", lambda e, o=bkt.ap[:, 0:128], i=cps:
                                    e.transpose(o, i, ident_sb[:]),
                                    [tcp, bkt.free], inc="pe")
                        cp_war[k % 2] = ttp
                        tev = em.op("dve", lambda e,
                                    o=ar(A_QT + k * 1024 + tc * 128, 128),
                                    i=bkt.ap[:, 0:128]:
                                    e.tensor_copy(o, i), [ttp], inc="dve")
                        bkt.free = tev
                    t_ofm = ("dve", em.cnt["dve"])
                    for n in range(2):
                        bk = P[(tc % 2) * 2 + n]
                        items = []
                        for k in range(KC):
                            items.append((bk.ap[:],
                                          ar(A_QT + k * 1024 + tc * 128, 128),
                                          ar(A_WA + k * 1024 + n * 512, 512)))
                        for i, (tgt, l, r) in enumerate(items):
                            w = [t_wo, t_ofm, bk.free] if i == 0 else ()
                            em.op("pe", lambda e, t=tgt, l=l, r=r, st=(i == 0):
                                  e.matmul(t, l, r, start=st, stop=False), w)
                        tok = em.op("pe", lambda e, t=bk.ap[:],
                                    r=bo_sb[0:1, n * 512:(n + 1) * 512]:
                                    e.matmul(t, ones_bf[0:1, :], r, start=False,
                                             stop=True), (), inc="pe")
                        bk.ready = tok
                        av = acc[:, tc * 1024 + n * 512:tc * 1024 + (n + 1) * 512]
                        tev = em.op("dve", lambda e, o=av, i=bk.ap[:]:
                                    e.tensor_tensor(o, o, i, ALU.add),
                                    [tok, t_main], inc="dve")
                        bk.free = tev

            if "L0" in layers and ph >= 2:
                prev_tex = None
                for tc in range(8):
                    bk = P[4 + tc % 2]
                    items = []
                    for k in range(KC):
                        items.append((bk.ap[:, 0:NE0],
                                      xn_sl(k, 128 + tc * 128, 128),
                                      wr_sb[:, k * NE0:(k + 1) * NE0]))
                    for i, (tgt, l, r) in enumerate(items):
                        w = [bk.free] if i == 0 else ()
                        em.op("pe", lambda e, t=tgt, l=l, r=r, st=(i == 0):
                              e.matmul(t, l, r, start=st, stop=False), w)
                    tok = em.op("pe", lambda e, t=bk.ap[:, 0:NE0],
                                r=br_sb[0:1, :]:
                                e.matmul(t, ones_bf[0:1, :], r, start=False,
                                         stop=True), (), inc="pe")
                    bk.ready = tok
                    trr = em.op("dve", lambda e, o=r_scr[:, 0:NE0], i=bk.ap[:, 0:NE0]:
                                e.tensor_copy(o, i), [tok, prev_tex], inc="dve")
                    bk.free = trr
                    dsp()
                    em.op("dve", lambda e, o=m8[:], i=r_scr[:, 0:NE0]:
                          e.max(o, i))
                    dsp()
                    tsub = em.op("dve", lambda e, o=wstat[:, 0:1], a=m8[:, 0:1],
                                 b=m8[:, 1:2]:
                                 e.tensor_scalar(o, a, b, None, ALU.subtract),
                                 inc="dve")
                    tex = em.op("act", lambda e, o=wstat[:, 1:2], i=wstat[:, 0:1]:
                                e.activation(o, i, AF.Exp, bias=0.0,
                                             scale=-1.0 / 1.5), [tsub], inc="act")
                    prev_tex = tex
                    em.op("dve", lambda e, o=wstat[:, 1:2], i=wstat[:, 1:2]:
                          e.tensor_scalar(o, i, 1.0, None, ALU.add), [tex])
                    dsp()
                    em.op("dve", lambda e, o=wstat[:, 2:3], i=wstat[:, 1:2]:
                          e.reciprocal(o, i))
                    dsp()
                    em.op("dve", lambda e, o=wstat[:, 3:4], i=wstat[:, 2:3]:
                          e.tensor_scalar(o, i, -1.0, 1.0, ALU.mult, ALU.add))
                    em.op("dve", lambda e, o=eqA[:, 0:NE0], i=r_scr[:, 0:NE0],
                          m=m8[:, 0:1]: e.tensor_scalar(o, i, m, None, ALU.is_equal))
                    dsp()
                    em.op("dve", lambda e, o=eqA[:, 0:NE0], i=eqA[:, 0:NE0],
                          s=wstat[:, 2:3]: e.tensor_scalar(o, i, s, None, ALU.mult))
                    em.op("dve", lambda e, o=eqB[:, 0:NE0], i=r_scr[:, 0:NE0],
                          m=m8[:, 1:2]: e.tensor_scalar(o, i, m, None, ALU.is_equal))
                    dsp()
                    em.op("dve", lambda e, o=wtok[:, tc * NE0:(tc + 1) * NE0],
                          i=eqB[:, 0:NE0], s=wstat[:, 3:4], a=eqA[:, 0:NE0]:
                          e.scalar_tensor_tensor(o, i, s, a, ALU.mult, ALU.add),
                          inc="dve")
                t_wtok = ("dve", em.cnt["dve"])

            if "L0" in layers and ph >= 3:
                ffn_barrier = em.snap()
                t_ws2 = em.dma("gp", ar(A_WS2, 16384).rearrange("p (f n) -> p f n", n=1024),
                               ws2.rearrange("(f p) n -> p f n", p=128), ffn_barrier, sem="w2s")
                ws1v = ws1.rearrange("(k p) (j x n) -> p j x k n", p=128, j=2, n=128)
                s1bufs = (ar(A_E1A, 2048), ar(A_E1B, 2048))
                s1free = [None, None]
                gwar = {0: None, 1: None}
                for f in range(16):
                    buf = s1bufs[f % 2]
                    t_w = em.dma("gp",
                                 buf.rearrange("p (j k n) -> p j k n", j=2, n=128),
                                 ws1v[:, :, f, :, :],
                                 [s1free[f % 2]] + (ffn_barrier if f < 2 else []),
                                 sem=("s1a", "s1b")[f % 2])
                    par = (f % 2) * 4
                    bksA = (P[par], P[par + 1])
                    bksG = (P[par + 2], P[par + 3])
                    for half, bks, jof in ((0, bksA, 0), (1, bksG, 1024)):
                        pe_inter(em, bks,
                                 (lambda k, b=buf, j=jof:
                                  b[:, j + k * 128:j + (k + 1) * 128]),
                                 (lambda k, tt: xn_sl(k, 128 + tt * 512, 512)),
                                 [t_w, t_xn])
                    s1free[f % 2] = ("pe", em.cnt["pe"])
                    for tt in range(2):
                        bkA, bkG = bksA[tt], bksG[tt]
                        gts = gt_scr[:, tt * 512:(tt + 1) * 512]
                        ats = at_scr[:, tt * 512:(tt + 1) * 512]
                        tg = em.op("act", lambda e, o=gts, i=bkG.ap[:],
                                   b=bs1_sb[:, 16 + f:17 + f]:
                                   e.activation(o, i, AF.Gelu, bias=b, scale=1.0),
                                   [bkG.ready, gwar[tt]], inc="act")
                        bkG.free = tg
                        ta2 = em.op("act", lambda e, o=ats, i=bkA.ap[:],
                                    b=bs1_sb[:, f:f + 1]:
                                    e.activation(o, i, AF.Identity, bias=b, scale=1.0),
                                    [bkA.ready, gwar[tt]], inc="act")
                        bkA.free = ta2
                        tm = em.op("dve", lambda e,
                                   o=ar(A_ACT + f * 1024 + tt * 512, 512),
                                   a=ats, g=gts:
                                   e.tensor_tensor(o, a, g, ALU.mult),
                                   [tg, ta2], inc="dve")
                        gwar[tt] = tm
                t_act_all = ("dve", em.cnt["dve"])
                for p2 in range(2):
                    for f in range(16):
                        for sub in range(4):
                            for n in range(2):
                                bk = P[sub * 2 + n]
                                l = ar(A_ACT + f * 1024 + p2 * 512 + sub * 128, 128)
                                r = ar(A_WS2 + f * 1024 + n * 512, 512)
                                w = [t_ws2, t_act_all, bk.free] if f == 0 else ()
                                em.op("pe", lambda e, t=bk.ap[:], l=l, r=r,
                                      st=(f == 0):
                                      e.matmul(t, l, r, start=st, stop=False), w)
                    for sub in range(4):
                        for n in range(2):
                            bk = P[sub * 2 + n]
                            tok = em.op("pe", lambda e, t=bk.ap[:],
                                        r=bs2_sb[0:1, n * 512:(n + 1) * 512]:
                                        e.matmul(t, ones_bf[0:1, :], r,
                                                 start=False, stop=True), (),
                                        inc="pe")
                            bk.ready = tok
                            tc = p2 * 4 + sub
                            av = acc[:, tc * 1024 + n * 512:tc * 1024 + (n + 1) * 512]
                            tev = em.op("dve", lambda e, o=av, i=bk.ap[:]:
                                        e.tensor_tensor(o, o, i, ALU.add),
                                        [tok], inc="dve")
                            bk.free = tev

            if "L0" in layers and ph >= 4:
                we1v = we1.rearrange("e (k p) (j x n) -> e p j x k n",
                                     p=128, j=2, n=128)
                we2v = we2.rearrange("e (f p) n -> e p f n", p=128)
                exp_barrier = em.snap()
                e1bufs = (ar(A_E1A, 2048), ar(A_E1B, 2048))
                e1free = [exp_barrier, exp_barrier]
                e2free = exp_barrier
                for ex in range(NE0):
                    for f in range(FE):
                        buf = e1bufs[f % 2]
                        t_w = em.dma("gp",
                                     buf.rearrange("p (j k n) -> p j k n", j=2, n=128),
                                     we1v[ex, :, :, f, :, :], [e1free[f % 2]],
                                     sem=("e1a", "e1b")[f % 2])
                        if f in (2, 4, 6):
                            gg = f // 2 - 1
                            em.dma("gp",
                                   ar(A_E2 + gg * 8192, 8192)
                                   .rearrange("p (f n) -> p f n", n=1024),
                                   we2v[ex, :, 8 * gg:8 * gg + 8, :],
                                   [e2free], sem="e2a")
                        if f == 8:
                            t_be2 = em.dma("gp", be2row[:], be2_sb[ex:ex + 1, :],
                                           [("dve", em.cnt["dve"])], sem="be2")
                        par = (f % 2) * 4
                        bksA = (P[par], P[par + 1])
                        bksG = (P[par + 2], P[par + 3])
                        for half, bks, jof in ((0, bksA, 0), (1, bksG, 1024)):
                            pe_inter(em, bks,
                                     (lambda k, b=buf, j=jof:
                                      b[:, j + k * 128:j + (k + 1) * 128]),
                                     (lambda k, tt: xn_sl(k, 128 + tt * 512, 512)),
                                     [t_w, t_xn])
                        e1free[f % 2] = ("pe", em.cnt["pe"])
                        for tt in range(2):
                            bkA, bkG = bksA[tt], bksG[tt]
                            gts = gt_scr[:, tt * 512:(tt + 1) * 512]
                            ats = at_scr[:, tt * 512:(tt + 1) * 512]
                            bc = ex * 48
                            tg = em.op("act", lambda e, o=gts, i=bkG.ap[:],
                                       b=be1_sb[:, bc + 24 + f:bc + 25 + f]:
                                       e.activation(o, i, AF.Gelu, bias=b, scale=1.0),
                                       [bkG.ready, gwar[tt]],
                                       inc="act")
                            bkG.free = tg
                            ta2 = em.op("act", lambda e, o=ats, i=bkA.ap[:],
                                        b=be1_sb[:, bc + f:bc + 1 + f]:
                                        e.activation(o, i, AF.Identity, bias=b, scale=1.0),
                                        [bkA.ready, gwar[tt]], inc="act")
                            bkA.free = ta2
                            tm = em.op("dve", lambda e,
                                       o=ar(A_ACT + f * 1024 + tt * 512, 512),
                                       a=ats, g=gts:
                                       e.tensor_tensor(o, a, g, ALU.mult),
                                       [tg, ta2], inc="dve")
                            gwar[tt] = tm
                    t_act_all = ("dve", em.cnt["dve"])
                    t_we2_tok = ("e2a", em.cnt["e2a"])
                    for p2 in range(2):
                        for f in range(FE):
                            for sub in range(4):
                                for n in range(2):
                                    bk = P[sub * 2 + n]
                                    l = ar(A_ACT + f * 1024 + p2 * 512 + sub * 128, 128)
                                    r = ar(A_E2 + f * 1024 + n * 512, 512)
                                    w = ([t_we2_tok, t_act_all, bk.free]
                                         if f == 0 else ())
                                    em.op("pe", lambda e, t=bk.ap[:], l=l, r=r,
                                          st=(f == 0):
                                          e.matmul(t, l, r, start=st, stop=False),
                                          w)
                        for sub in range(4):
                            for n in range(2):
                                bk = P[sub * 2 + n]
                                tok = em.op("pe", lambda e, t=bk.ap[:],
                                            r=be2row[0:1, n * 512:(n + 1) * 512]:
                                            e.matmul(t, ones_bf[0:1, :], r,
                                                     start=False, stop=True),
                                            [t_be2], inc="pe")
                                bk.ready = tok
                                tc = p2 * 4 + sub
                                av = acc[:, tc * 1024 + n * 512:
                                         tc * 1024 + (n + 1) * 512]
                                wc = wtok[:, tc * NE0 + ex:tc * NE0 + ex + 1]
                                tev = em.op("dve", lambda e, o=av, i=bk.ap[:], s=wc:
                                            e.scalar_tensor_tensor(o, i, s, o,
                                                                   ALU.mult, ALU.add),
                                            [tok, t_wtok], inc="dve")
                                bk.free = tev
                        if p2 == 1:
                            e2free = ("pe", em.cnt["pe"])

            if "L1" in layers:
                l1_barrier = em.snap()
                chunks = ln_chunks(q, False)
                if "L0" in layers:
                    tok_srcs = [("dve", em.cnt["dve"])] * 8
                else:
                    tok_srcs = [t_main] * 8
                emit_ln(q, chunks, tok_srcs, update_acc=True)
                t_xn1 = ("dve", em.cnt["dve"])

                t_cb = em.dma("gp", ar(A_CB, 8192).rearrange("p (i j) -> p i j", j=1024),
                              cbind.rearrange("(i p) j -> p i j", p=128), l1_barrier, sem="cb")
                t_cc = em.dma("gp", ar(A_CC, 8192).rearrange("p (i j) -> p i j", j=1024),
                              ccorr.rearrange("(i p) j -> p i j", p=128), l1_barrier, sem="cc")
                for j in range(KC):
                    par = j % 2
                    bks = (P[par * 2], P[par * 2 + 1])
                    bscr = f32p[:, 2 * 1024:3 * 1024]
                    for tt in range(2):
                        bk = bks[tt]
                        items = []
                        for iC in range(KC):
                            items.append((bk.ap[:],
                                          ar(A_CB + iC * 1024 + j * 128, 128),
                                          xn_sl(iC, 128 + tt * 512, 512)))
                        tok = pe_group(em, bk, items, [t_cb, t_xn1])
                        tev = em.op("dve", lambda e, o=bscr[:, tt * 512:(tt + 1) * 512],
                                    i=bk.ap[:]: e.tensor_copy(o, i), [tok],
                                    inc="dve")
                        bk.free = tev
                    dsp()
                    tsc = em.op("dve", lambda e, o=f32p[:, 3072:4096], d0=dtile[:], d1=bscr,
                                ini=carries[:, j:j + 1]:
                                e.tensor_tensor_scan(o, d0, d1, ini, ALU.mult,
                                                     ALU.add), [t_dtile], inc="dve")
                    dsp()
                    em.op("dve", lambda e, o=carries[:, j:j + 1],
                          i=f32p[:, 4095:4096]: e.tensor_copy(o, i))
                    em.op("dve", lambda e, o=ar(A_HOLO + j * 1024, 1024), i=f32p[:, 3072:4096]:
                          e.tensor_copy(o, i), inc="dve")
                t_holo = ("dve", em.cnt["dve"])
                for tc in range(8):
                    par = tc % 2
                    bks = (P[4 + par * 2], P[5 + par * 2])
                    for n in range(2):
                        bk = bks[n]
                        items = []
                        for iC in range(KC):
                            items.append((bk.ap[:],
                                          ar(A_HOLO + iC * 1024 + tc * 128, 128),
                                          ar(A_CC + iC * 1024 + n * 512, 512)))
                        tok = pe_group(em, bk, items, [t_cc, t_holo])
                        av = acc[:, tc * 1024 + n * 512:tc * 1024 + (n + 1) * 512]
                        tev = em.op("dve", lambda e, o=av, i=bk.ap[:], s=grep[:]:
                                    e.scalar_tensor_tensor(o, i, s, o, ALU.mult,
                                                           ALU.add),
                                    [tok, t_grep], inc="dve")
                        bk.free = tev

                prev_tex1 = None
                for tc in range(8):
                    bk = P[tc % 2]
                    items = []
                    for k in range(KC):
                        items.append((bk.ap[:, 0:16],
                                      xn_sl(k, 128 + tc * 128, 128),
                                      rwt_sb[:, k * 16:(k + 1) * 16]))
                    tok = pe_group(em, bk, items, [t_rwt, t_xn1])
                    trr = em.op("dve", lambda e, o=r_scr[:], i=bk.ap[:, 0:16]:
                                e.tensor_copy(o, i), [tok, prev_tex1], inc="dve")
                    bk.free = trr
                    dsp()
                    tmx = em.op("dve", lambda e, o=wstat[:, 0:1], i=r_scr[:]:
                                e.reduce_max(o, i, AX.X, negate=True), inc="dve")
                    tex = em.op("act", lambda e, o=r_scr[:], i=r_scr[:],
                                b=wstat[:, 0:1]:
                                e.activation(o, i, AF.Exp, bias=b, scale=1.0),
                                [tmx], inc="act")
                    prev_tex1 = tex
                    em.op("dve", lambda e, o=wstat[:, 1:2], i=r_scr[:]:
                          e.reduce_sum(o, i, AX.X), [tex])
                    dsp()
                    em.op("dve", lambda e, o=wstat[:, 2:3], i=wstat[:, 1:2]:
                          e.reciprocal(o, i))
                    dsp()
                    em.op("dve", lambda e, o=r_scr[:], i=r_scr[:], s=wstat[:, 2:3]:
                          e.tensor_scalar(o, i, s, None, ALU.mult))
                    dsp()
                    em.op("dve", lambda e, o=m8[:], i=r_scr[:]: e.max(o, i))
                    dsp()
                    em.op("dve", lambda e, o=eqA[:], i=r_scr[:], t4=m8[:, 3:4]:
                          e.tensor_scalar(o, i, t4, None, ALU.is_ge))
                    dsp()
                    em.op("dve", lambda e, o=gates[:, tc * 16:(tc + 1) * 16],
                          a=r_scr[:], b=eqA[:]:
                          e.tensor_tensor(o, a, b, ALU.mult), inc="dve")
                t_gates = ("dve", em.cnt["dve"])

                supv = supt.rearrange("s (k p) e -> s p k e", p=128)
                sdnv = sdnt.rearrange("s (f p) n -> s p f n", p=128)
                rupv = rupt.rearrange("s (k p) e -> s p k e", p=128)
                rdnv = rdnt.rearrange("s (f p) n -> s p f n", p=128)
                upbufs = (ar(A_SUPA, 4096), ar(A_SUPB, 4096))
                dnbufs = (ar(A_SDNA, 4096), ar(A_SDNB, 4096))
                upfree = [None, None]
                dnfree = [None, None]
                ffn1_barrier = em.snap()
                for xi in range(18):
                    par = xi % 2
                    if xi < 2:
                        upsrc, dnsrc, gcol = supv[xi], sdnv[xi], None
                    else:
                        r = xi - 2
                        upsrc, dnsrc, gcol = rupv[r], rdnv[r], r
                    t_up = em.dma("gp",
                                  upbufs[par].rearrange("p (k e) -> p k e", e=512),
                                  upsrc, [upfree[par]] + (ffn1_barrier if xi < 2 else []),
                                  sem=("ua", "ub")[par])
                    t_dn = em.dma("gp",
                                  dnbufs[par].rearrange("p (f n) -> p f n", n=1024),
                                  dnsrc, [dnfree[par]] + (ffn1_barrier if xi < 2 else []),
                                  sem=("da", "db")[par])
                    tok_g = {}
                    for ec in range(4):
                        for tt in range(2):
                            bk = P[(ec % 2) * 2 + tt]
                            items = []
                            for k in range(KC):
                                items.append((bk.ap[:],
                                              upbufs[par][:, k * 512 + ec * 128:
                                                          k * 512 + (ec + 1) * 128],
                                              xn_sl(k, 128 + tt * 512, 512)))
                            tok = pe_group(em, bk, items, [t_up, t_xn1])
                            tg = em.op("act", lambda e,
                                       o=ar(A_ACTR + ec * 1024 + tt * 512, 512),
                                       i=bk.ap[:]:
                                       e.activation(o, i, AF.Gelu, bias=0.0,
                                                    scale=1.0), [tok], inc="act")
                            bk.free = tg
                            tok_g[(ec, tt)] = tg
                    upfree[par] = ("act", em.cnt["act"])
                    t_actr = ("act", em.cnt["act"])
                    for p2 in range(2):
                        for sub in range(4):
                            for n in range(2):
                                bk = P[sub * 2 + n]
                                items = []
                                for ec in range(4):
                                    items.append((bk.ap[:],
                                                  ar(A_ACTR + ec * 1024 + p2 * 512
                                                     + sub * 128, 128),
                                                  dnbufs[par][:, ec * 1024 + n * 512:
                                                              ec * 1024 + (n + 1) * 512]))
                                tok = pe_group(em, bk, items,
                                               [t_dn, t_actr])
                                tc = p2 * 4 + sub
                                av = acc[:, tc * 1024 + n * 512:
                                         tc * 1024 + (n + 1) * 512]
                                if gcol is None:
                                    tev = em.op("dve", lambda e, o=av, i=bk.ap[:]:
                                                e.tensor_tensor(o, o, i, ALU.add),
                                                [tok], inc="dve")
                                else:
                                    gc = gates[:, tc * 16 + gcol:tc * 16 + gcol + 1]
                                    tev = em.op("dve", lambda e, o=av, i=bk.ap[:],
                                                s=gc:
                                                e.scalar_tensor_tensor(
                                                    o, i, s, o, ALU.mult, ALU.add),
                                                [tok, t_gates], inc="dve")
                                bk.free = tev
                    dnfree[par] = ("dve", em.cnt["dve"])

            store_barrier = em.snap()
            em.dma("sp", dst[qs:qs + QT, :].rearrange("(c p) n -> p c n", p=128),
                   acc[:].rearrange("p (c n) -> p c n", n=1024), store_barrier, sem="st")

        if dump:
            dump_o = nc.dram_tensor("dump_arena", [128, 57344], BF16,
                                    kind="ExternalOutput").ap()
            dump_x = nc.dram_tensor("dump_xn", [128, KC * XNW], BF16,
                                    kind="ExternalOutput").ap()
            em.limit = None
            snapf = em.snap()
            em.dma("sp", dump_o, arena[:], snapf, sem="st")
            em.dma("sp", dump_x, xn[:], snapf, sem="st")
        for s in ("dio", "sx", "st"):
            em.final_wait("sp", s)
        for s in ("dwm", "wW", "s1a", "s1b", "w2s", "e1a", "e1b", "e2a",
                  "e2b", "be2", "cb", "cc", "ua", "ub", "da", "db"):
            em.final_wait("gp", s)

        @block.sync
        def _(eng):
            for f in em.ops["sp"]:
                f(eng, sems)

        @block.gpsimd
        def _(eng):
            for f in em.ops["gp"]:
                f(eng, sems)

        @block.tensor
        def _(eng):
            for f in em.ops["pe"]:
                f(eng, sems)

        @block.scalar
        def _(eng):
            for f in em.ops["act"]:
                f(eng, sems)

        @block.vector
        def _(eng):
            for f in em.ops["dve"]:
                f(eng, sems)

    return nc


def _host_prep(inputs):
    g = {k: np.asarray(v, dtype=np.float32) for k, v in inputs.items()}
    idxf = np.arange(D)
    roll_conv = (idxf[None, :] - idxf[:, None]) % D
    roll_corr = (idxf[:, None] - idxf[None, :]) % D
    c = lambda a: np.ascontiguousarray(a, dtype=np.float32)
    common = dict(
        wq=c(g["L0_attn_Wq"]), wk=c(g["L0_attn_Wk"]),
        wv=c(g["L0_attn_Wv"]), wo=c(g["L0_attn_Wo"]),
        ws1=c(g["L0_ffn_Ws1"]), ws2=c(g["L0_ffn_Ws2"]),
        we1=c(g["L0_ffn_We1"]), we2=c(g["L0_ffn_We2"]),
        wr=c(g["L0_ffn_Wr"]),
        bq_c=c(g["L0_attn_bq"].reshape(8, 128).T),
        bk_c=c(g["L0_attn_bk"].reshape(8, 128).T),
        bs1_c=c(g["L0_ffn_bs1"].reshape(32, 128).T),
        be1_c=c(g["L0_ffn_be1"].reshape(NE0, 48, 128).transpose(2, 0, 1)
                .reshape(128, NE0 * 48)),
        bv_r=c(g["L0_attn_bv"].reshape(1, D)),
        bo_r=c(g["L0_attn_bo"].reshape(1, D)),
        bs2_r=c(g["L0_ffn_bs2"].reshape(1, D)),
        br_r=c(g["L0_ffn_br"].reshape(1, NE0)),
        be2=c(g["L0_ffn_be2"]),
        cbind=c(g["L1_attn_bind_key"][roll_conv]),
        ccorr=c(g["L1_attn_unbind_key"][roll_corr]),
        supt=c(g["L1_ffn_shared_up"].transpose(0, 2, 1)),
        sdnt=c(g["L1_ffn_shared_down"].transpose(0, 2, 1)),
        rupt=c(g["L1_ffn_routed_up"].transpose(0, 2, 1)),
        rdnt=c(g["L1_ffn_routed_down"].transpose(0, 2, 1)),
        rwt=c(g["L1_ffn_router_w"].T),
        decay_rep=c(np.broadcast_to(g["L1_attn_decay"].reshape(1, 1), (128, 1))),
        gate_rep=c(np.broadcast_to(g["L1_attn_gate"].reshape(1, 1), (128, 1))),
    )
    x = c(g["L0_attn_x"])
    return common, x


def kernel(**inputs):
    common, x = _host_prep(inputs)
    ncores = 8
    if FUSED:
        nc = build_nc(("L0", "L1"))
        in_maps = [dict(common, src=x[b]) for b in range(ncores)]
        res = run_bass_kernel_spmd(nc, in_maps, core_ids=list(range(ncores)))
        out = np.stack([np.asarray(res.results[b]["dst"]) for b in range(ncores)])
    else:
        nc0 = build_nc(("L0",))
        in_maps = [dict(common, src=x[b]) for b in range(ncores)]
        r0 = run_bass_kernel_spmd(nc0, in_maps, core_ids=list(range(ncores)))
        mid = [np.asarray(r0.results[b]["dst"]) for b in range(ncores)]
        nc1 = build_nc(("L1",))
        in_maps = [dict(common, src=mid[b]) for b in range(ncores)]
        r1 = run_bass_kernel_spmd(nc1, in_maps, core_ids=list(range(ncores)))
        out = np.stack([np.asarray(r1.results[b]["dst"]) for b in range(ncores)])
    return out.astype(np.float32)
